# version 11
# baseline (speedup 1.0000x reference)
import numpy as np
from contextlib import ExitStack

import concourse.bass as bass
import concourse.mybir as mybir
from concourse.bass_utils import run_bass_kernel_spmd

F32 = mybir.dt.float32
I32 = mybir.dt.int32
BF16 = mybir.dt.bfloat16
AF = mybir.ActivationFunctionType
ALU = mybir.AluOpType
AX = mybir.AxisListType

S, SQ, E, H, DH, NG, NE, NKC = 4096, 2048, 1024, 16, 64, 8, 8, 32
CSLACK = 1.0703125


class Sched:
    ENG = ("sync", "vector", "scalar", "tensor", "gpsimd")

    def __init__(self, nc):
        self.nc = nc
        self.steps = []
        self.dma_sems = set()

    def step(self, engine, fn, sem=None, waits=()):
        assert engine in self.ENG
        sem = sem or ("e_" + engine)
        ws = []
        for w in waits:
            if w is None:
                continue
            assert 0 <= w < len(self.steps), (w, len(self.steps))
            ws.append(w)
        self.steps.append(dict(engine=engine, fn=fn, sem=sem, waits=ws))
        return len(self.steps) - 1

    def dma(self, queue, out, in_, sem, waits=()):
        self.dma_sems.add(sem)
        return self.step(queue, lambda e, o=out, i=in_: e.dma_start(out=o, in_=i),
                         sem=sem, waits=waits)

    def barrier(self, engine, waits):
        fn = (lambda e: e.engine_nop()) if engine in ("vector", "gpsimd") else None
        return self.step(engine, fn, waits=waits)

    def emit(self, block, ctx):
        used = set()
        for st_ in self.steps:
            used.update(st_["waits"])
        for i in used:
            assert self.steps[i]["fn"] is not None, f"referenced barrier step {i}"
        sem_val = {}
        for i, st_ in enumerate(self.steps):
            inc = 16 if st_["sem"] in self.dma_sems else (1 if i in used else 0)
            st_["inc"] = inc
            st_["val_after"] = sem_val.get(st_["sem"], 0) + inc
            sem_val[st_["sem"]] = st_["val_after"]
        sems = {n: ctx.enter_context(self.nc.semaphore("sem_" + n)) for n in sem_val}
        steps = self.steps
        dma_sems = self.dma_sems

        def runner(name):
            own = "e_" + name

            def section(eng):
                wm = {}
                for st_ in steps:
                    if st_["engine"] != name:
                        continue
                    need = {}
                    for w in st_["waits"]:
                        ws = steps[w]
                        need[ws["sem"]] = max(need.get(ws["sem"], 0), ws["val_after"])
                    for tsem, tval in need.items():
                        if wm.get(tsem, 0) >= tval:
                            continue
                        eng.wait_ge(sems[tsem], tval)
                        wm[tsem] = tval
                    if st_["fn"] is None:
                        continue
                    inst = st_["fn"](eng)
                    if st_["inc"]:
                        inst.then_inc(sems[st_["sem"]], st_["inc"])
            return section

        block.sync(runner("sync"))
        block.vector(runner("vector"))
        block.scalar(runner("scalar"))
        block.tensor(runner("tensor"))
        block.gpsimd(runner("gpsimd"))


def build_program():
    nc = bass.Bass("TRN2", target_bir_lowering=False)

    def din(name, shape):
        return nc.dram_tensor(name, list(shape), F32, kind="ExternalInput").ap()

    X = din("X", (S, E))
    XOWN = din("XOWN", (SQ, E))
    WQT = din("WQT", (8, 128, 8, 128))
    WKT = din("WKT", (8, 128, 8, 128))
    WVT = din("WVT", (8, 128, 8, 128))
    WOT = din("WOT", (8, 128, 8, 128))
    W1T = din("W1T", (8, 2, 128, 4096))
    W2T = din("W2T", (8, 2, 128, 4096))
    BQ, BK, BV = din("BQ", (128, 8)), din("BK", (128, 8)), din("BV", (128, 8))
    BOUT = din("BOUT", (1, E))
    WG, BG = din("WG", (128, 64)), din("BG", (8, 1))
    B1, B2 = din("B1", (128, 64)), din("B2", (128, 64))
    OUT = nc.dram_tensor("OUT", [SQ, E], F32, kind="ExternalOutput").ap()
    XTH = nc.dram_tensor("XTH", [8, 128, 8, 512], BF16, kind="Internal").ap()
    XTL = nc.dram_tensor("XTL", [8, 128, 8, 512], BF16, kind="Internal").ap()
    XOH = nc.dram_tensor("XOH", [4, 128, 8, 512], BF16, kind="Internal").ap()
    XOL = nc.dram_tensor("XOL", [4, 128, 8, 512], BF16, kind="Internal").ap()
    XTO = nc.dram_tensor("XTO", [4, 128, 8, 512], F32, kind="Internal").ap()
    ATH = nc.dram_tensor("ATH", [4, 128, 8, 512], BF16, kind="Internal").ap()
    ATL = nc.dram_tensor("ATL", [4, 128, 8, 512], BF16, kind="Internal").ap()
    X1D = nc.dram_tensor("X1D", [4, 128, 8, 512], F32, kind="Internal").ap()

    ctx = ExitStack()
    pool = ctx.enter_context(nc.sbuf_tensor("pool", [128, 52960], F32))
    ident = ctx.enter_context(nc.sbuf_tensor("ident", [128, 128], F32))
    PD = [ctx.enter_context(nc.psum_tensor(f"pd{i}", [128, 1024], F32))
          for i in range(4)]
    block = ctx.enter_context(nc.Block())
    sch = Sched(nc)
    st, dma = sch.step, sch.dma

    P_ONES, P_HM = 50176, 50688
    P_BQ, P_BK, P_BV = 50692, 50700, 50708
    P_B1, P_B2, P_WG = 50716, 50780, 50844
    P_BG, P_BOUT = 50908, 50912
    P_ESEL = 51936
    SM = 48128

    def sl(off, n, p0=0, p1=128):
        return pool[p0:p1, off:off + n]

    def ones(p0, p1, n):
        return pool[p0:p1, P_ONES:P_ONES + n]

    def bsl(off, nbf, p0=0, p1=128):
        return pool[p0:p1, off:off + nbf // 2].bitcast(BF16)

    P_NEGC = P_HM + 2

    t_id0 = st("gpsimd", lambda e: e.memset(ident[:], 0.0))
    t_id = st("gpsimd", lambda e: e.affine_select(
        out=ident[:], in_=ident[:], compare_op=ALU.not_equal, fill=1.0,
        base=0, pattern=[[-1, 128]], channel_multiplier=1), waits=[t_id0])
    t_on = st("gpsimd", lambda e: e.memset(sl(P_ONES, 512), 1.0))
    t_h0 = st("gpsimd", lambda e: e.memset(sl(P_HM, 2), 0.0))
    t_h1 = st("gpsimd", lambda e: e.memset(sl(P_HM, 1, 0, 64), 1.0), waits=[t_h0])
    t_h2 = st("gpsimd", lambda e: e.memset(sl(P_HM + 1, 1, 64, 128), 1.0), waits=[t_h1])
    t_es0 = st("gpsimd", lambda e: e.memset(pool[0:8, P_ESEL:P_ESEL + 1024], 0.0))
    t_es = [t_es0, st("gpsimd", lambda e: e.affine_select(
        out=pool[0:8, P_ESEL:P_ESEL + 1024].rearrange("p (a m) -> p a m", a=8),
        in_=pool[0:8, P_ESEL:P_ESEL + 1024].rearrange("p (a m) -> p a m", a=8),
        compare_op=ALU.not_equal, fill=1.0, base=0,
        pattern=[[-1, 8], [0, 128]], channel_multiplier=1), waits=[t_es0])]
    tb = [dma("sync", dst, src, "d_c") for dst, src in (
        (sl(P_BQ, 8), BQ), (sl(P_BK, 8), BK), (sl(P_BV, 8), BV),
        (sl(P_B1, 64), B1), (sl(P_B2, 64), B2), (sl(P_WG, 64), WG),
        (sl(P_BG, 1, 0, 8), BG), (sl(P_BOUT, 1024, 0, 1), BOUT))]
    t_consts = sch.barrier("gpsimd", [t_id, t_on, t_h2] + t_es + tb)

    XIN, XSTH, XSTL, XSTF = 0, 8192, 10240, 12288
    xin_busy = [None, None]
    prev_ev = []
    prev_wr = []
    all_wr = []
    for tb_i in range(12):
        own = tb_i >= 8
        src, tb0 = (X, tb_i) if not own else (XOWN, tb_i - 8)
        bi = tb_i % 2
        xin = sl(XIN + bi * 4096, 4096)
        tl = dma("sync", xin.rearrange("p (tt d) -> p tt d", tt=4),
                 src[tb0 * 512:(tb0 + 1) * 512, :].rearrange("(tt p) d -> p tt d", p=128),
                 "d_x%d" % bi, waits=[xin_busy[bi], t_consts])
        ev_half = [None, None]
        blk_ev = []
        for tt in range(4):
            tps = []
            for dc in range(8):
                half = dc // 4
                tp = st("tensor", lambda e, tt=tt, dc=dc, half=half, xin=xin: e.transpose(
                    PD[0][:, half * 512 + (dc % 4) * 128: half * 512 + (dc % 4) * 128 + 128],
                    xin[:, tt * 1024 + dc * 128: tt * 1024 + dc * 128 + 128], ident[:]),
                    waits=[tl, t_id, ev_half[half]] + (prev_ev[-2:] if (tt == 0 and dc < 2) else []))
                tps.append(tp)
            for half in range(2):
                hi = st("scalar", lambda e, tt=tt, half=half: e.activation(
                    bsl(XSTH + half * 1024, 2048).rearrange("p (c t) -> p c t", c=4)
                      [:, :, tt * 128:tt * 128 + 128],
                    PD[0][:, half * 512:half * 512 + 512].rearrange("p (c t) -> p c t", c=4),
                    AF.Copy), waits=[tps[half * 4 + 3]] + (prev_wr if tt == 0 else []))
                if own:
                    f3 = st("vector", lambda e, tt=tt, half=half: e.tensor_copy(
                        sl(XSTF + half * 2048, 2048).rearrange("p (c t) -> p c t", c=4)
                          [:, :, tt * 128:tt * 128 + 128],
                        PD[0][:, half * 512:half * 512 + 512].rearrange("p (c t) -> p c t", c=4)),
                        waits=[hi])
                else:
                    f3 = hi
                lo = st("vector", lambda e, tt=tt, half=half: e.tensor_sub(
                    bsl(XSTL + half * 1024, 2048).rearrange("p (c t) -> p c t", c=4)
                      [:, :, tt * 128:tt * 128 + 128],
                    PD[0][:, half * 512:half * 512 + 512].rearrange("p (c t) -> p c t", c=4),
                    bsl(XSTH + half * 1024, 2048).rearrange("p (c t) -> p c t", c=4)
                      [:, :, tt * 128:tt * 128 + 128]),
                    waits=[hi, f3])
                ev_half[half] = lo
                blk_ev.append(lo)
        xin_busy[bi] = tps[-1]
        wrs = []
        dsth = XOH if own else XTH
        dstl = XOL if own else XTL
        wrs.append(dma("scalar", dsth[tb0],
                       bsl(XSTH, 4096).rearrange("p (c t) -> p c t", c=8),
                       "d_sc", waits=[blk_ev[-2], blk_ev[-1]]))
        wrs.append(dma("scalar", dstl[tb0],
                       bsl(XSTL, 4096).rearrange("p (c t) -> p c t", c=8),
                       "d_sc", waits=[blk_ev[-2], blk_ev[-1]]))
        if own:
            wrs.append(dma("scalar", XTO[tb0],
                           sl(XSTF, 4096).rearrange("p (c t) -> p c t", c=8),
                           "d_sc", waits=[blk_ev[-2], blk_ev[-1]]))
        prev_ev, prev_wr = blk_ev, wrs
        all_wr += wrs
    t_p0 = sch.barrier("gpsimd", all_wr)

    XTB = 0
    VSTG, ATMP, RBC = 8192, 8192, 10240
    AQH, AQL, AQF = 10752, 11008, 11264
    WSTG = 12288
    WS0, WS1 = 13312, 16384
    QG, KG = 19456, 21504
    VT = 25600
    KTMP, QSQ = 29760, 30272
    ABUF = 30784
    SUMS = SM

    vtok_ones = [st("gpsimd", lambda e, hh=hh: e.memset(
        pool[:, VT + hh * 2080 + 64: VT + hh * 2080 + 2080:65], 1.0),
        waits=[t_consts]) for hh in range(2)]

    attn_wr = []
    prev_group = [t_p0] + vtok_ones
    prev_group_mm = [t_p0]
    w_busy = [None, None]
    aq_busy = None
    prev_exps = None

    for g in range(NG):
        wslot = WS0 if g % 2 == 0 else WS1
        wsem = "d_w%d" % (g % 2)
        tw = []
        prev_wstg = None
        for wi, wsrc in enumerate((WQT, WKT, WVT)):
            wl = dma("sync", sl(WSTG, 1024).rearrange("p (c m) -> p c m", c=8),
                     wsrc[g], wsem,
                     waits=[w_busy[g % 2], prev_wstg, t_p0])
            sh = st("vector", lambda e, wi=wi, wslot=wslot: e.tensor_copy(
                bsl(wslot + wi * 512, 1024), sl(WSTG, 1024)), waits=[wl])
            so = st("vector", lambda e, wi=wi, wslot=wslot: e.tensor_sub(
                bsl(wslot + 1536 + wi * 512, 1024), sl(WSTG, 1024),
                bsl(wslot + wi * 512, 1024)), waits=[sh])
            tw.append(so)
            prev_wstg = so

        t_nrm0 = st("gpsimd", lambda e: e.memset(sl(SM, 2, 0, 2), 0.0),
                    waits=prev_group)
        qkv_ev = []
        bank_busy = [None] * 4
        bank_i = [0]
        norm_last = [t_nrm0]
        xt_busy = [None, None]
        qkv_last_mm = [None]
        ktmp_busy = [t_nrm0]

        def qkv_tile(src_h, src_l, t, parts, g=g, tw=tw):
            bi = t % 2
            xth = bsl(XTB + bi * 4096, 4096).rearrange("p (c k) -> p c k", c=8)
            xtl = bsl(XTB + bi * 4096 + 2048, 4096).rearrange("p (c k) -> p c k", c=8)
            tl = dma("sync", xth, src_h[t], "d_xt%d" % bi, waits=[xt_busy[bi], t_p0])
            tl2 = dma("sync", xtl, src_l[t], "d_xt%d" % bi, waits=[xt_busy[bi], t_p0])
            for (wcol, ooff, osz, bcol, wantsq) in parts:
                b = bank_i[0] % 4
                bank_i[0] += 1
                bank = PD[b // 2][:, (b % 2) * 512:(b % 2) * 512 + 512]
                mms = []
                for dc in range(8):
                    for lh, rh in ((0, xth), (0, xtl), (1536, xth)):
                        mm = st("tensor", lambda e, bank=bank, wcol=wcol, dc=dc, rh=rh,
                                lh=lh, wslot=wslot: e.matmul(
                            bank, bsl(wslot + lh + wcol + dc * 64, 128), rh[:, dc, :],
                            start=(dc == 0 and lh == 0 and rh is xth),
                            stop=(dc == 7 and lh == 1536)),
                            waits=[tl, tl2, tw[-1], bank_busy[b]] + prev_group_mm)
                        mms.append(mm)
                if ooff == VSTG:
                    ev = st("vector", lambda e, bank=bank, t=t, bcol=bcol: e.tensor_scalar(
                        sl(VSTG + t * 512, 512), bank, sl(bcol, 1), None, ALU.add),
                        waits=[mms[-1]] + prev_group)
                    qkv_ev.append(ev)
                    bank_busy[b] = ev
                else:
                    ev = st("vector", lambda e, bank=bank, bcol=bcol: e.tensor_scalar(
                        sl(KTMP, 512), bank, sl(bcol, 1), None, ALU.add),
                        waits=[mms[-1]] + ktmp_busy)
                    hi = st("vector", lambda e, ooff=ooff, t=t: e.tensor_copy(
                        bsl(ooff + t * 256, 512), sl(KTMP, 512)), waits=[ev])
                    lo = st("vector", lambda e, ooff=ooff, osz=osz, t=t: e.tensor_sub(
                        bsl(ooff + osz + t * 256, 512), sl(KTMP, 512),
                        bsl(ooff + t * 256, 512)), waits=[hi])
                    qkv_ev.append(lo)
                    bank_busy[b] = ev
                    sq = st("scalar", lambda e: e.activation(
                        sl(QSQ, 512), sl(KTMP, 512), AF.Square),
                        waits=[ev, norm_last[0]])
                    nm = st("tensor", lambda e: e.matmul(
                        PD[3][0:2, 0:512], sl(P_HM, 2), sl(QSQ, 512),
                        start=True, stop=True), waits=[sq, norm_last[0]])
                    rm = st("vector", lambda e: e.reduce_max(
                        sl(SM + 2, 1, 0, 2), PD[3][0:2, 0:512], axis=AX.X), waits=[nm])
                    col = 0 if ooff == QG else 1
                    mx = st("vector", lambda e, col=col: e.tensor_max(
                        sl(SM + col, 1, 0, 2), sl(SM + col, 1, 0, 2),
                        sl(SM + 2, 1, 0, 2)), waits=[rm, t_nrm0])
                    norm_last[0] = mx
                    ktmp_busy[:] = [hi, lo, sq]
                qkv_last_mm[0] = mms[-1]
            xt_busy[bi] = qkv_last_mm[0]

        for t in range(8):
            qkv_tile(XTH, XTL, t, [(512, KG, 2048, P_BK + g, True),
                                   (1024, VSTG, 0, P_BV + g, False)])
        for t in range(4):
            qkv_tile(XOH, XOL, t, [(0, QG, 1024, P_BQ + g, True)])

        v_ev = []
        vt_busy = [None, None]
        for kc in range(NKC):
            sls = kc % 2
            soff = sls * 512
            tp = st("tensor", lambda e, kc=kc, soff=soff: e.transpose(
                PD[3][:, soff: soff + 128],
                sl(VSTG + kc * 128, 128), ident[:]),
                waits=[qkv_ev[-1], t_id, vt_busy[sls]] + ([norm_last[0]] if kc < 2 else []))
            evs = []
            for hh in range(2):
                ev = st("vector", lambda e, kc=kc, hh=hh, soff=soff: e.tensor_copy(
                    sl(VT + hh * 2080 + kc * 65, 64),
                    PD[3][:, soff + hh * 64: soff + hh * 64 + 64]),
                    waits=[tp] + vtok_ones + prev_group)
                evs.append(ev)
                v_ev.append(ev)
            vt_busy[sls] = evs[-1]

        t_q1 = st("tensor", lambda e: e.transpose(
            PD[3][0:1, 512 + 256:512 + 258], pool[0:2, SM:SM + 1], ident[0:2, 0:2]),
            waits=[norm_last[0], t_id, v_ev[-1], v_ev[-2]])
        t_k1 = st("tensor", lambda e: e.transpose(
            PD[3][0:1, 512 + 258:512 + 260], pool[0:2, SM + 1:SM + 2], ident[0:2, 0:2]),
            waits=[norm_last[0], t_id, t_q1])
        e_q1 = st("vector", lambda e: e.tensor_copy(pool[0:1, SM + 4:SM + 6],
                                                    PD[3][0:1, 512 + 256:512 + 258]),
                  waits=[t_q1])
        e_k1 = st("vector", lambda e: e.tensor_copy(pool[0:1, SM + 6:SM + 8],
                                                    PD[3][0:1, 512 + 258:512 + 260]),
                  waits=[t_k1])
        t_qk = st("vector", lambda e: e.tensor_mul(pool[0:1, SM + 8:SM + 10],
                                                   pool[0:1, SM + 4:SM + 6],
                                                   pool[0:1, SM + 6:SM + 8]),
                  waits=[e_q1, e_k1])
        t_c2 = st("scalar", lambda e: e.activation(pool[0:1, SM + 10:SM + 12],
                                                   pool[0:1, SM + 8:SM + 10], AF.Sqrt),
                  waits=[t_qk])
        t_negc = st("vector", lambda e: e.tensor_scalar_mul(
            pool[0:1, SM + 10:SM + 12], pool[0:1, SM + 10:SM + 12], -CSLACK / 8.0),
            waits=[t_c2])

        prev_negc_ev = None
        for hh in range(2):
            h64 = hh * 64
            gh = 2 * g + hh
            bcm = st("tensor", lambda e, hh=hh: e.matmul(
                PD[3][:, 512 + 260 + hh: 512 + 261 + hh], ones(0, 1, 128),
                pool[0:1, SM + 10 + hh:SM + 11 + hh], start=True, stop=True),
                waits=[t_negc, t_on, t_k1, prev_negc_ev])
            negc_ev = st("vector", lambda e, hh=hh: e.tensor_copy(
                sl(P_NEGC + hh, 1), PD[3][:, 512 + 260 + hh:512 + 261 + hh]),
                waits=[bcm])
            prev_negc_ev = negc_ev

            qb_sums, last_av_ev = [], None
            prev_qb_av = {0: [], 1: []}
            for qb in range(4):
                exp_toks = []
                av_mms = []
                avb = PD[2][:, (qb % 2) * 512:(qb % 2) * 512 + 512]
                for kcb in range(16):
                    smms = []
                    for j in range(2):
                        kc = kcb * 2 + j
                        for lh, rq in ((0, 0), (0, 1024), (2048, 0)):
                            mm = st("tensor", lambda e, kc=kc, kcb=kcb, j=j, h64=h64,
                                    qb=qb, lh=lh, rq=rq: e.matmul(
                                PD[kcb % 2][:, j * 512:j * 512 + 512],
                                bsl(KG + lh + kc * 64, 128, h64, h64 + 64),
                                bsl(QG + rq + qb * 256, 512, h64, h64 + 64),
                                start=(lh == 0 and rq == 0), stop=(lh == 2048)),
                                waits=[exp_toks[kcb - 2] if kcb >= 2 else
                                       (prev_exps[14 + kcb] if (kcb < 2 and prev_exps) else None),
                                       qkv_ev[-1], negc_ev])
                            smms.append(mm)
                    if kcb >= 1:
                        pk = kcb - 1
                        for j in range(2):
                            kc = pk * 2 + j
                            avm = st("tensor", lambda e, kc=kc, hh=hh, avb=avb: e.matmul(
                                avb[0:65, :], sl(VT + hh * 2080 + kc * 65, 65),
                                sl(ABUF + kc * 512, 512),
                                start=(kc == 0), stop=(kc == 31)),
                                waits=[exp_toks[pk], v_ev[-1]] +
                                      (prev_qb_av[qb % 2] if kc == 0 else []))
                            av_mms.append(avm)
                    ex = st("scalar", lambda e, kcb=kcb, hh=hh: e.activation(
                        sl(ABUF + kcb * 1024, 1024), PD[kcb % 2][:, 0:1024],
                        AF.Exp, bias=sl(P_NEGC + hh, 1), scale=0.125),
                        waits=[smms[-1], negc_ev])
                    exp_toks.append(ex)
                for j in range(2):
                    kc = 30 + j
                    avm = st("tensor", lambda e, kc=kc, hh=hh, avb=avb: e.matmul(
                        avb[0:65, :], sl(VT + hh * 2080 + kc * 65, 65),
                        sl(ABUF + kc * 512, 512), start=False, stop=(kc == 31)),
                        waits=[exp_toks[15]])
                    av_mms.append(avm)
                av_last_mm = avm
                evq = st("scalar", lambda e, qb=qb, avb=avb: e.activation(
                    pool[0:64, ATMP + qb * 512:ATMP + qb * 512 + 512],
                    avb[0:64, :], AF.Copy), waits=[av_mms[-1], aq_busy])
                smv = st("vector", lambda e, qb=qb, avb=avb: e.tensor_copy(
                    pool[64:65, SUMS + qb * 512:SUMS + qb * 512 + 512],
                    avb[64:65, :]), waits=[av_mms[-1], aq_busy, evq])
                qb_sums.append(smv)
                prev_qb_av[qb % 2] = [evq, smv]
                last_av_ev = evq
                prev_exps = exp_toks
            rc = st("vector", lambda e: e.reciprocal(
                pool[64:65, SUMS:SUMS + 2048], pool[64:65, SUMS:SUMS + 2048]),
                waits=qb_sums)
            aw_prev = [None]
            rb_busy = None
            for qb in range(4):
                bm = st("tensor", lambda e, qb=qb: e.matmul(
                    PD[3][:, 0:512], ones(64, 65, 128),
                    pool[64:65, SUMS + qb * 512:SUMS + qb * 512 + 512],
                    start=True, stop=True),
                    waits=[rc, t_on, rb_busy, last_av_ev])
                rbev = st("vector", lambda e: e.tensor_copy(sl(RBC, 512), PD[3][:, 0:512]),
                          waits=[bm, aw_prev[0]])
                m_hi = st("vector", lambda e, qb=qb: e.tensor_mul(
                    bsl(AQH, 512, 0, 64),
                    pool[0:64, ATMP + qb * 512:ATMP + qb * 512 + 512],
                    sl(RBC, 512, 0, 64)), waits=[rbev] + (aw_prev[0:1] if aw_prev[0] else [])
                    + [last_av_ev])
                m_f = st("vector", lambda e, qb=qb: e.tensor_mul(
                    sl(AQF, 512, 0, 64),
                    pool[0:64, ATMP + qb * 512:ATMP + qb * 512 + 512],
                    sl(RBC, 512, 0, 64)), waits=[m_hi])
                m_lo = st("vector", lambda e: e.tensor_sub(
                    bsl(AQL, 512, 0, 64), sl(AQF, 512, 0, 64), bsl(AQH, 512, 0, 64)),
                    waits=[m_f])
                w1 = dma("scalar", ATH[qb, (gh % 2) * 64:(gh % 2) * 64 + 64, gh // 2, :],
                         bsl(AQH, 512, 0, 64), "d_at", waits=[m_hi])
                w2 = dma("scalar", ATL[qb, (gh % 2) * 64:(gh % 2) * 64 + 64, gh // 2, :],
                         bsl(AQL, 512, 0, 64), "d_at", waits=[m_lo, w1])
                aw_prev[0] = w2
                rb_busy = rbev
            attn_wr.append(aw_prev[0])
            aq_busy = aw_prev[0]
        w_busy[g % 2] = qkv_last_mm[0]
        prev_group = [attn_wr[-1], attn_wr[-2]]
        prev_group_mm = [av_last_mm]

    AR = 0
    WOS, WOH, WOL = 8192, 9216, 9728
    XO2, MBC, SBC, X1S = 10240, 11264, 13312, 15360
    RSD, SQT, GS = 17408, 33792, 34304
    ST_P2 = (38912, 40960, 43008)

    def stat(slots, i, cols=slice(0, 2048)):
        return pool[0:1, slots[i] + cols.start: slots[i] + cols.stop]

    res_ev = {}
    stat_ev = []
    sum_mms = {}
    wo_split = None
    wo_stg_busy = None
    xo_busy = {0: None, 1: None}
    ar_busy = {0: None, 1: None}
    for tt in range(4):
        arh = bsl(AR + (tt % 2) * 4096, 4096).rearrange("p (c t) -> p c t", c=8)
        arl = bsl(AR + (tt % 2) * 4096 + 2048, 4096).rearrange("p (c t) -> p c t", c=8)
        al = dma("sync", arh, ATH[tt], "d_ar%d" % (tt % 2),
                 waits=attn_wr + [ar_busy[tt % 2]])
        al2 = dma("sync", arl, ATL[tt], "d_ar%d" % (tt % 2),
                  waits=attn_wr + [ar_busy[tt % 2]])
        last_mm_set = None
        for oc in range(8):
            wl = dma("scalar", sl(WOS, 1024).rearrange("p (c m) -> p c m", c=8),
                     WOT[oc], "d_wo",
                     waits=[wo_stg_busy] if wo_stg_busy else [attn_wr[-1]])
            wh = st("vector", lambda e: e.tensor_copy(bsl(WOH, 1024), sl(WOS, 1024)),
                    waits=[wl, last_mm_set])
            wo = st("vector", lambda e: e.tensor_sub(bsl(WOL, 1024), sl(WOS, 1024),
                                                     bsl(WOH, 1024)), waits=[wh])
            wo_stg_busy = wo
            xl = dma("sync", sl(XO2 + (oc % 2) * 512, 512),
                     XTO[tt, :, oc, :], "d_xo%d" % (oc % 2),
                     waits=[xo_busy[oc % 2], t_p0])
            b = oc % 4
            bank = PD[b // 2][:, (b % 2) * 512:(b % 2) * 512 + 512]
            mms = []
            for dc in range(8):
                for pi, (lh, rh) in enumerate(((WOH, arh), (WOH, arl), (WOL, arh))):
                    mm = st("tensor", lambda e, bank=bank, dc=dc, rh=rh, lh=lh,
                            st0=(dc == 0 and pi == 0): e.matmul(
                        bank, bsl(lh + dc * 64, 128), rh[:, dc, :],
                        start=st0, stop=False),
                        waits=[al, al2, wo,
                               res_ev.get((tt, oc - 4), res_ev.get((tt - 1, oc + 4)))])
                    mms.append(mm)
            bmm = st("tensor", lambda e, bank=bank, oc=oc: e.matmul(
                bank, pool[0:1, P_BOUT + oc * 128:P_BOUT + oc * 128 + 128],
                ones(0, 1, 512), start=False, stop=True), waits=[mms[-1], t_on])
            rev = st("vector", lambda e, bank=bank, oc=oc, tt=tt: e.scalar_tensor_tensor(
                sl(RSD + oc * 2048 + tt * 512, 512), bank, 1.0,
                sl(XO2 + (oc % 2) * 512, 512), ALU.mult, ALU.add), waits=[bmm, xl])
            res_ev[(tt, oc)] = rev
            wo_busy = bmm
            xo_busy[oc % 2] = rev
            sq = st("scalar", lambda e, oc=oc, tt=tt: e.activation(
                sl(SQT, 512), sl(RSD + oc * 2048 + tt * 512, 512), AF.Square),
                waits=[rev, sum_mms.get((tt, oc - 1, 1))])
            m1 = st("tensor", lambda e, oc=oc, tt=tt: e.matmul(
                PD[2][0:1, 0:512], ones(0, 128, 1),
                sl(RSD + oc * 2048 + tt * 512, 512),
                start=(oc == 0), stop=(oc == 7)),
                waits=[rev] + ([stat_ev[-2]] if (oc == 0 and stat_ev) else []))
            m2 = st("tensor", lambda e, oc=oc: e.matmul(
                PD[2][0:1, 512:1024], ones(0, 128, 1), sl(SQT, 512),
                start=(oc == 0), stop=(oc == 7)),
                waits=[sq] + ([stat_ev[-1]] if (oc == 0 and stat_ev) else []))
            sum_mms[(tt, oc, 0)] = m1
            sum_mms[(tt, oc, 1)] = m2
            last_mm_set = bmm
        ar_busy[tt % 2] = last_mm_set
        e1 = st("vector", lambda e, tt=tt: e.tensor_copy(
            stat(ST_P2, 0, slice(tt * 512, (tt + 1) * 512)), PD[2][0:1, 0:512]),
            waits=[sum_mms[(tt, 7, 0)]])
        e2 = st("vector", lambda e, tt=tt: e.tensor_copy(
            stat(ST_P2, 1, slice(tt * 512, (tt + 1) * 512)), PD[2][0:1, 512:1024]),
            waits=[sum_mms[(tt, 7, 1)]])
        stat_ev += [e1, e2]

    def ln_stats(sts, mbc, sbc, waits, bcast_extra):
        t1 = st("vector", lambda e: e.tensor_scalar_mul(stat(sts, 0), stat(sts, 0),
                                                        1.0 / 1024), waits=waits)
        t2 = st("vector", lambda e: e.tensor_scalar_mul(stat(sts, 1), stat(sts, 1),
                                                        1.0 / 1024), waits=waits)
        t3 = st("vector", lambda e: e.tensor_mul(stat(sts, 2), stat(sts, 0),
                                                 stat(sts, 0)), waits=[t1])
        t4 = st("vector", lambda e: e.tensor_sub(stat(sts, 1), stat(sts, 1),
                                                 stat(sts, 2)), waits=[t2, t3])
        t5 = st("vector", lambda e: e.tensor_scalar_add(stat(sts, 1), stat(sts, 1),
                                                        1e-5), waits=[t4])
        t6 = st("scalar", lambda e: e.activation(stat(sts, 2), stat(sts, 1), AF.Sqrt),
                waits=[t5, t3])
        t7 = st("vector", lambda e: e.reciprocal(stat(sts, 2), stat(sts, 2)),
                waits=[t6])
        bcev = []
        prev_bm = None
        for r, dst in ((0, mbc), (2, sbc)):
            for q in range(4):
                bm = st("tensor", lambda e, r=r, q=q: e.matmul(
                    PD[3][:, 0:512], ones(0, 1, 128),
                    stat(sts, r, slice(q * 512, q * 512 + 512)),
                    start=True, stop=True),
                    waits=[t7 if r == 2 else t1, t_on,
                           bcev[-1] if bcev else None] + bcast_extra)
                ev = st("vector", lambda e, dst=dst, q=q: e.tensor_copy(
                    sl(dst + q * 512, 512), PD[3][:, 0:512]),
                    waits=[bm, bcev[-1] if bcev else None])
                bcev.append(ev)
                prev_bm = bm
        return [t7] + bcev

    ln1 = ln_stats(ST_P2, MBC, SBC, stat_ev, [attn_wr[-1]])
    x1_wr = []
    for dc in range(8):
        evs = []
        for tt in range(4):
            a = st("vector", lambda e, dc=dc, tt=tt: e.tensor_sub(
                sl(X1S + tt * 512, 512), sl(RSD + dc * 2048 + tt * 512, 512),
                sl(MBC + tt * 512, 512)),
                waits=ln1 + [res_ev[(tt, dc)], x1_wr[-1] if x1_wr else None])
            b = st("vector", lambda e, tt=tt: e.tensor_mul(
                sl(X1S + tt * 512, 512), sl(X1S + tt * 512, 512),
                sl(SBC + tt * 512, 512)), waits=[a])
            evs.append(b)
        x1_wr.append(dma("scalar", X1D[:, :, dc, :].rearrange("q p t -> p q t"),
                         sl(X1S, 2048).rearrange("p (q t) -> p q t", q=4),
                         "d_x1", waits=evs))

    t_io = st("gpsimd", lambda e: e.iota(
        sl(GS + 256, 64).bitcast(I32).rearrange("p (a b) -> p a b", a=8),
        pattern=[[1, 8], [-1, 8]], base=0, channel_multiplier=0),
        waits=[t_consts, attn_wr[-1]])
    t_iof = st("vector", lambda e: e.tensor_copy(
        sl(GS + 320, 64), sl(GS + 256, 64).bitcast(I32)), waits=[t_io])
    t_lt = st("vector", lambda e: e.tensor_single_scalar(
        sl(GS + 256, 64), sl(GS + 320, 64), 0.0, ALU.is_gt), waits=[t_iof])

    gate_sc = []
    g_ld = {0: None, 1: None}
    for tt in range(4):
        x1b = sl(AR + (tt % 2) * 4096, 4096).rearrange("p (c t) -> p c t", c=8)
        xl = dma("sync", x1b, X1D[tt], "d_ar%d" % (tt % 2),
                 waits=x1_wr + [g_ld[tt % 2] if g_ld[tt % 2] else ar_busy[tt % 2]])
        mms = []
        for dc in range(8):
            mm = st("tensor", lambda e, dc=dc, x1b=x1b: e.matmul(
                PD[0][0:8, 0:512], sl(P_WG + dc * 8, 8), x1b[:, dc, :],
                start=(dc == 0), stop=(dc == 7)),
                waits=[xl, gate_sc[-1] if gate_sc else None,
                       res_ev[(3, 4)] if (tt == 0 and dc == 0) else None,
                       res_ev[(3, 5)] if (tt == 0 and dc == 0) else None])
            mms.append(mm)
        ev = st("vector", lambda e, tt=tt: e.tensor_scalar(
            pool[0:8, GS + 512 + tt * 512:GS + 512 + (tt + 1) * 512],
            PD[0][0:8, 0:512], sl(P_BG, 1, 0, 8), None, ALU.add), waits=[mms[-1]])
        gate_sc.append(ev)
        g_ld[tt % 2] = mms[-1]
    g_tp = []
    for c in range(16):
        tp = st("tensor", lambda e, c=c: e.transpose(
            PD[1][:, c * 8:c * 8 + 8],
            pool[0:8, GS + 512 + c * 128:GS + 512 + c * 128 + 128],
            ident[0:8, 0:8]),
            waits=[gate_sc[c // 4], t_id,
                   res_ev[(3, 6)] if c == 0 else None,
                   res_ev[(3, 7)] if c == 0 else None])
        g_tp.append(tp)
    g_ev1 = st("vector", lambda e: e.tensor_copy(sl(GS, 128), PD[1][:, 0:128]),
               waits=[g_tp[-1]])
    g_ev2 = g_ev1
    sc3 = sl(GS, 128).rearrange("p (c e) -> p c e", c=16)
    r1 = st("vector", lambda e: e.tensor_tensor(
        sl(GS + 2560, 1024).rearrange("p (c a b) -> p c a b", c=16, a=8),
        sc3.unsqueeze(2).to_broadcast([128, 16, 8, 8]),
        sc3.unsqueeze(3).to_broadcast([128, 16, 8, 8]), ALU.is_gt),
        waits=[g_ev1, g_ev2, t_lt])
    r2 = st("vector", lambda e: e.tensor_tensor(
        sl(GS + 3584, 1024).rearrange("p (c a b) -> p c a b", c=16, a=8),
        sc3.unsqueeze(2).to_broadcast([128, 16, 8, 8]),
        sc3.unsqueeze(3).to_broadcast([128, 16, 8, 8]), ALU.is_equal),
        waits=[g_ev1, g_ev2, r1])
    r3 = st("vector", lambda e: e.tensor_mul(
        sl(GS + 3584, 1024).rearrange("p (c a b) -> p c a b", c=16, a=8),
        sl(GS + 3584, 1024).rearrange("p (c a b) -> p c a b", c=16, a=8),
        sl(GS + 256, 64).rearrange("p (a b) -> p a b", a=8)
          .unsqueeze(1).to_broadcast([128, 16, 8, 8])), waits=[r2])
    r4 = st("vector", lambda e: e.tensor_add(
        sl(GS + 2560, 1024), sl(GS + 2560, 1024), sl(GS + 3584, 1024)), waits=[r3])
    r5 = st("vector", lambda e: e.tensor_reduce(
        sl(GS + 128, 128).rearrange("p (c e) -> p c e", c=16),
        sl(GS + 2560, 1024).rearrange("p (c a b) -> p c a b", c=16, a=8),
        axis=AX.X, op=ALU.add), waits=[r4])
    r6 = st("vector", lambda e: e.tensor_single_scalar(
        sl(GS + 128, 128), sl(GS + 128, 128), 2.0, ALU.is_lt), waits=[r5])
    r7 = st("vector", lambda e: e.tensor_mul(
        sl(GS + 128, 128), sl(GS + 128, 128), sl(GS, 128)), waits=[r6])
    wt_tp = []
    wt_ev = []
    for c in range(16):
        rnd, pos = c // 4, c % 4
        tp = st("tensor", lambda e, c=c, rnd=rnd, pos=pos: e.transpose(
            PD[1][0:8, (rnd % 2) * 512 + pos * 128:(rnd % 2) * 512 + pos * 128 + 128],
            sl(GS + 128 + c * 8, 8), ident[:]),
            waits=[r7, t_id, wt_ev[rnd - 2] if rnd >= 2 and pos == 0 else None,
                   wt_tp[-1] if wt_tp else g_ev2])
        wt_tp.append(tp)
        if pos == 3:
            ev = st("vector", lambda e, rnd=rnd: e.tensor_copy(
                pool[0:8, SM + rnd * 512:SM + rnd * 512 + 512],
                PD[1][0:8, (rnd % 2) * 512:(rnd % 2) * 512 + 512]),
                waits=[tp, gate_sc[-1]])
            wt_ev.append(ev)
    t_wt = sch.barrier("vector", wt_ev)
    WTO = SM

    X1Q = 0
    XQH, XQL = 4096, 6144
    HTMP, F2T = 8192, 8704
    HSH, HSL = 9216, 11264
    WSC = 13312
    W1H, W1L = 15360, 19456
    W2H, W2L = 23552, 27648
    ACC = 31744

    def bsl(off, nbf, p0=0, p1=128):
        return pool[p0:p1, off:off + nbf // 2].bitcast(BF16)

    acc_ev = {}
    hs_ev = {}
    wsc_ev = {}
    prev_f1_mm = [t_wt]
    prev_f2_mm = [t_wt]
    prev_wld = None
    prev_ae = None
    prev_xsplit = None
    prev_hl = None
    for ei in range(NE):
        wsplits = []
        for wt_dram, whi, wlo in ((W1T, W1H, W1L), (W2T, W2H, W2L)):
            gate_mm = prev_f1_mm if wt_dram is W1T else prev_f2_mm
            for half in range(2):
                wl = dma("sync", sl(X1Q, 4096), wt_dram[ei, half],
                         "d_f%d" % (half % 2),
                         waits=[prev_wld if prev_wld else t_wt] +
                               (gate_mm if half == 0 else []) +
                               (wsplits[-1:] if wsplits else []))
                prev_wld = wl
                sh = st("vector", lambda e, whi=whi, half=half: e.tensor_copy(
                    bsl(whi + half * 2048, 4096), sl(X1Q, 4096)), waits=[wl])
                so = st("vector", lambda e, whi=whi, wlo=wlo, half=half: e.tensor_sub(
                    bsl(wlo + half * 2048, 4096), sl(X1Q, 4096),
                    bsl(whi + half * 2048, 4096)), waits=[sh])
                wsplits.append(so)
        for q in range(4):
            bm = st("tensor", lambda e, ei=ei, q=q: e.matmul(
                PD[3][:, 0:512], pool[0:8, P_ESEL + ei * 128:P_ESEL + ei * 128 + 128],
                pool[0:8, WTO + q * 512:WTO + q * 512 + 512],
                start=True, stop=True),
                waits=[t_wt, t_on, wsc_ev.get((ei, q - 1), wsc_ev.get((ei - 1, 3)))])
            ev = st("vector", lambda e, q=q: e.tensor_copy(
                sl(WSC + q * 512, 512), PD[3][:, 0:512]),
                waits=[bm, acc_ev.get((ei - 1, q, 7))])
            wsc_ev[(ei, q)] = ev
        for q in range(4):
            xq = sl(X1Q, 4096).rearrange("p (c t) -> p c t", c=8)
            xl = dma("sync", xq, X1D[q], "d_q",
                     waits=x1_wr + prev_f1_mm + wsplits[-1:])
            xh = st("vector", lambda e: e.tensor_copy(bsl(XQH, 4096), sl(X1Q, 4096)),
                    waits=[xl] + prev_f1_mm)
            xo = st("vector", lambda e: e.tensor_sub(bsl(XQL, 4096), sl(X1Q, 4096),
                                                     bsl(XQH, 4096)), waits=[xh])
            prev_xsplit = xo
            f1_mms = []
            for hc in range(8):
                bank = PD[0][:, (hc % 2) * 512:(hc % 2) * 512 + 512]
                mms = []
                for dc in range(8):
                    for lh, rh in ((W1H, XQH), (W1H, XQL), (W1L, XQH)):
                        mm = st("tensor", lambda e, bank=bank, hc=hc, dc=dc, lh=lh, rh=rh: e.matmul(
                            bank, bsl(lh + hc * 512 + dc * 64, 128),
                            bsl(rh + dc * 256, 512),
                            start=(dc == 0 and lh == W1H and rh == XQH),
                            stop=(dc == 7 and lh == W1L)),
                            waits=[xo, wsplits[hc * 2 + 1] if False else wsplits[-1],
                                   hs_ev.get((ei, q, hc - 2)),
                                   hs_ev.get((ei, q - 1, hc + 6)) if hc < 2 else None,
                                   hs_ev.get((ei - 1, 3, hc + 6)) if hc < 2 else None])
                        mms.append(mm)
                hv = st("scalar", lambda e, bank=bank, hc=hc, ei=ei: e.activation(
                    bsl(HSH + hc * 256, 512), bank, AF.Relu,
                    bias=sl(P_B1 + ei * 8 + hc, 1)),
                    waits=[mms[-1]] + prev_f2_mm[-1:])
                ht = st("vector", lambda e, bank=bank, hc=hc, ei=ei: e.tensor_scalar(
                    sl(HTMP, 512), bank, sl(P_B1 + ei * 8 + hc, 1), 0.0,
                    ALU.add, ALU.max), waits=[hv, prev_hl])
                hl = st("vector", lambda e, hc=hc: e.tensor_sub(
                    bsl(HSL + hc * 256, 512), sl(HTMP, 512), bsl(HSH + hc * 256, 512)),
                    waits=[ht])
                prev_hl = hl
                hs_ev[(ei, q, hc)] = hl
                f1_mms.append(mms[-1])
            prev_f1_mm = f1_mms[-2:]
            f2_mms = []
            for dc in range(8):
                bank = PD[1][:, (dc % 2) * 512:(dc % 2) * 512 + 512]
                mms = []
                for hc in range(8):
                    for lh, rh in ((W2H, HSH), (W2H, HSL), (W2L, HSH)):
                        mm = st("tensor", lambda e, bank=bank, dc=dc, hc=hc, lh=lh, rh=rh: e.matmul(
                            bank, bsl(lh + dc * 512 + hc * 64, 128),
                            bsl(rh + hc * 256, 512),
                            start=(hc == 0 and lh == W2H and rh == HSH),
                            stop=(hc == 7 and lh == W2L)),
                            waits=[hs_ev[(ei, q, hc)],
                                   acc_ev.get((ei, q, dc - 2),
                                              acc_ev.get((ei, q - 1, dc + 6),
                                                         acc_ev.get((ei - 1, 3, dc + 6))))
                                   if hc == 0 else None])
                        mms.append(mm)
                if ei == 0:
                    ae = st("vector", lambda e, bank=bank, dc=dc, q=q, ei=ei: e.scalar_tensor_tensor(
                        sl(ACC + dc * 2048 + q * 512, 512), bank,
                        sl(P_B2 + ei * 8 + dc, 1), sl(WSC + q * 512, 512),
                        ALU.add, ALU.mult), waits=[mms[-1], wsc_ev[(ei, q)]])
                else:
                    t1 = st("vector", lambda e, bank=bank, dc=dc, q=q, ei=ei: e.scalar_tensor_tensor(
                        sl(F2T, 512), bank, sl(P_B2 + ei * 8 + dc, 1),
                        sl(WSC + q * 512, 512), ALU.add, ALU.mult),
                        waits=[mms[-1], wsc_ev[(ei, q)], prev_ae])
                    ae = st("vector", lambda e, dc=dc, q=q: e.tensor_add(
                        sl(ACC + dc * 2048 + q * 512, 512),
                        sl(ACC + dc * 2048 + q * 512, 512), sl(F2T, 512)),
                        waits=[t1, acc_ev[(ei - 1, q, dc)]])
                    prev_ae = ae
                acc_ev[(ei, q, dc)] = ae
                f2_mms.append(mms[-1])
            prev_f2_mm = f2_mms[-2:]

    MBC2, SBC2 = 15360, 17408
    ST_P3b = (19456, 21504, 23552)
    X2S, OUTS = 25600, 29696
    r2_ev = {}
    st2_mms = {}
    stat2_ev = []
    for q in range(4):
        xq = sl(X1Q, 4096).rearrange("p (c t) -> p c t", c=8)
        xl = dma("sync", xq, X1D[q], "d_q",
                 waits=[acc_ev[(NE - 1, q, dc)] for dc in range(8)] +
                 prev_f1_mm + [prev_xsplit] + ([r2_ev[(q - 1, 7)]] if q else []))
        for dc in range(8):
            rv = st("vector", lambda e, dc=dc, q=q: e.tensor_add(
                sl(ACC + dc * 2048 + q * 512, 512),
                sl(ACC + dc * 2048 + q * 512, 512), xq[:, dc, :]),
                waits=[xl, acc_ev[(NE - 1, q, dc)]])
            r2_ev[(q, dc)] = rv
            sq = st("scalar", lambda e, dc=dc, q=q: e.activation(
                sl(F2T, 512), sl(ACC + dc * 2048 + q * 512, 512), AF.Square),
                waits=[rv, st2_mms.get((q, dc - 1, 1)),
                       st2_mms.get((q - 1, 7, 1)) if dc == 0 else None,
                       prev_ae if (q == 0 and dc == 0) else None])
            m1 = st("tensor", lambda e, dc=dc, q=q: e.matmul(
                PD[2][0:1, 0:512], ones(0, 128, 1),
                sl(ACC + dc * 2048 + q * 512, 512),
                start=(dc == 0), stop=(dc == 7)),
                waits=[rv] + ([stat2_ev[-2]] if (dc == 0 and stat2_ev) else
                              ([sum_mms[(3, 7, 0)]] if dc == 0 else [])))
            m2 = st("tensor", lambda e, dc=dc: e.matmul(
                PD[2][0:1, 512:1024], ones(0, 128, 1), sl(F2T, 512),
                start=(dc == 0), stop=(dc == 7)),
                waits=[sq] + ([stat2_ev[-1]] if (dc == 0 and stat2_ev) else []))
            st2_mms[(q, dc, 0)] = m1
            st2_mms[(q, dc, 1)] = m2
        e1 = st("vector", lambda e, q=q: e.tensor_copy(
            stat(ST_P3b, 0, slice(q * 512, (q + 1) * 512)), PD[2][0:1, 0:512]),
            waits=[st2_mms[(q, 7, 0)], ln1[0], wsc_ev[(NE - 1, 3)]])
        e2 = st("vector", lambda e, q=q: e.tensor_copy(
            stat(ST_P3b, 1, slice(q * 512, (q + 1) * 512)), PD[2][0:1, 512:1024]),
            waits=[st2_mms[(q, 7, 1)], ln1[0]])
        stat2_ev += [e1, e2]

    ln2 = ln_stats(ST_P3b, MBC2, SBC2, stat2_ev, [prev_f1_mm[-1], prev_f2_mm[-1]])

    out_wr = []
    otp_busy = [None, None]
    for q in range(4):
        ap_ev = []
        for dc in range(8):
            a = st("vector", lambda e, dc=dc, q=q: e.tensor_sub(
                sl(X2S + dc * 512, 512), sl(ACC + dc * 2048 + q * 512, 512),
                sl(MBC2 + q * 512, 512)),
                waits=ln2 + [r2_ev[(q, dc)]] + out_wr[-1:])
            b = st("vector", lambda e, dc=dc, q=q: e.tensor_mul(
                sl(X2S + dc * 512, 512), sl(X2S + dc * 512, 512),
                sl(SBC2 + q * 512, 512)), waits=[a])
            ap_ev.append(b)
        for stt in range(4):
            tps = []
            for dc in range(8):
                half = dc // 4
                tp = st("tensor", lambda e, dc=dc, stt=stt, half=half: e.transpose(
                    PD[3][:, half * 512 + (dc % 4) * 128: half * 512 + (dc % 4) * 128 + 128],
                    sl(X2S + dc * 512 + stt * 128, 128), ident[:]),
                    waits=[ap_ev[dc], t_id, otp_busy[half],
                           ln2[-1] if (stt == 0 and dc == 0) else None])
                tps.append(tp)
            oe = st("vector", lambda e: e.tensor_copy(sl(OUTS, 1024), PD[3][:, 0:1024]),
                    waits=[tps[3], tps[7], out_wr[-1] if out_wr else None])
            otp_busy = [oe, oe]
            ow = dma("sync", OUT[q * 512 + stt * 128: q * 512 + stt * 128 + 128, :],
                     sl(OUTS, 1024), "d_out", waits=[oe])
            out_wr.append(ow)

    sch.barrier("sync", out_wr)
    sch.barrier("gpsimd", [out_wr[-1]])
    sch.emit(block, ctx)
    ctx.close()
    return nc


_NC = None


def kernel(**inputs):
    global _NC
    X = np.ascontiguousarray(np.asarray(inputs["L0_attn_X"], dtype=np.float32))
    Wqkv = np.asarray(inputs["L0_attn_W_qkv"], dtype=np.float32)
    bqkv = np.asarray(inputs["L0_attn_b_qkv"], dtype=np.float32)
    Wout = np.asarray(inputs["L0_attn_W_out"], dtype=np.float32)
    bout = np.asarray(inputs["L0_attn_b_out"], dtype=np.float32)
    Wg = np.asarray(inputs["L0_ffn_W_gate"], dtype=np.float32)
    bg = np.asarray(inputs["L0_ffn_b_gate"], dtype=np.float32)
    W1 = np.asarray(inputs["L0_ffn_W1"], dtype=np.float32)
    b1 = np.asarray(inputs["L0_ffn_b1"], dtype=np.float32)
    W2 = np.asarray(inputs["L0_ffn_W2"], dtype=np.float32)
    b2 = np.asarray(inputs["L0_ffn_b2"], dtype=np.float32)

    idx = (np.arange(H)[:, None] * 192 + np.arange(DH)[None, :]).ravel()
    WQ, WK, WV = Wqkv[:, idx], Wqkv[:, idx + 64], Wqkv[:, idx + 128]

    def t2(w):
        return np.ascontiguousarray(w.reshape(8, 128, 8, 128).transpose(2, 1, 0, 3))

    def t3(w):
        return np.ascontiguousarray(
            w.reshape(8, 8, 128, 2, 4, 128).transpose(0, 3, 2, 4, 1, 5)
            .reshape(8, 2, 128, 4096))

    shared = dict(
        WQT=t2(WQ), WKT=t2(WK), WVT=t2(WV), WOT=t2(Wout),
        W1T=t3(W1), W2T=t3(W2),
        BQ=np.ascontiguousarray(bqkv[idx].reshape(8, 128).T),
        BK=np.ascontiguousarray(bqkv[idx + 64].reshape(8, 128).T),
        BV=np.ascontiguousarray(bqkv[idx + 128].reshape(8, 128).T),
        BOUT=bout.reshape(1, E),
        WG=np.ascontiguousarray(Wg.reshape(8, 128, 8).transpose(1, 0, 2).reshape(128, 64)),
        BG=bg.reshape(8, 1),
        B1=np.ascontiguousarray(b1.reshape(8, 8, 128).transpose(2, 0, 1).reshape(128, 64)),
        B2=np.ascontiguousarray(b2.reshape(8, 8, 128).transpose(2, 0, 1).reshape(128, 64)),
    )
    if _NC is None:
        _NC = build_program()
    in_maps = []
    for c in range(8):
        b, half = c // 2, c % 2
        m = dict(shared)
        m["X"] = X[b]
        m["XOWN"] = np.ascontiguousarray(X[b, half * SQ:(half + 1) * SQ, :])
        in_maps.append(m)
    res = run_bass_kernel_spmd(_NC, in_maps, core_ids=list(range(8)))
    out = np.empty((4, S, E), dtype=np.float32)
    for c in range(8):
        b, half = c // 2, c % 2
        out[b, half * SQ:(half + 1) * SQ, :] = res.results[c]["OUT"]
    return out


# version 12
# speedup vs baseline: 1.0661x; 1.0661x over previous
import numpy as np
from contextlib import ExitStack

import concourse.bass as bass
import concourse.mybir as mybir
from concourse.bass_utils import run_bass_kernel_spmd

F32 = mybir.dt.float32
I32 = mybir.dt.int32
BF16 = mybir.dt.bfloat16
AF = mybir.ActivationFunctionType
ALU = mybir.AluOpType
AX = mybir.AxisListType

S, SQ, E, H, DH, NG, NE, NKC = 4096, 2048, 1024, 16, 64, 8, 8, 32
CSLACK = 1.0703125


class Sched:
    ENG = ("sync", "vector", "scalar", "tensor", "gpsimd")

    def __init__(self, nc):
        self.nc = nc
        self.steps = []
        self.dma_sems = set()

    def step(self, engine, fn, sem=None, waits=()):
        assert engine in self.ENG
        sem = sem or ("e_" + engine)
        ws = []
        for w in waits:
            if w is None:
                continue
            assert 0 <= w < len(self.steps), (w, len(self.steps))
            ws.append(w)
        self.steps.append(dict(engine=engine, fn=fn, sem=sem, waits=ws))
        return len(self.steps) - 1

    def dma(self, queue, out, in_, sem, waits=()):
        self.dma_sems.add(sem)
        return self.step(queue, lambda e, o=out, i=in_: e.dma_start(out=o, in_=i),
                         sem=sem, waits=waits)

    def barrier(self, engine, waits):
        fn = (lambda e: e.engine_nop()) if engine in ("vector", "gpsimd") else None
        return self.step(engine, fn, waits=waits)

    def emit(self, block, ctx):
        used = set()
        for st_ in self.steps:
            used.update(st_["waits"])
        for i in used:
            assert self.steps[i]["fn"] is not None, f"referenced barrier step {i}"
        sem_val = {}
        for i, st_ in enumerate(self.steps):
            inc = 16 if st_["sem"] in self.dma_sems else (1 if i in used else 0)
            st_["inc"] = inc
            st_["val_after"] = sem_val.get(st_["sem"], 0) + inc
            sem_val[st_["sem"]] = st_["val_after"]
        sems = {n: ctx.enter_context(self.nc.semaphore("sem_" + n)) for n in sem_val}
        steps = self.steps
        dma_sems = self.dma_sems

        def runner(name):
            own = "e_" + name

            def section(eng):
                wm = {}
                for st_ in steps:
                    if st_["engine"] != name:
                        continue
                    need = {}
                    for w in st_["waits"]:
                        ws = steps[w]
                        need[ws["sem"]] = max(need.get(ws["sem"], 0), ws["val_after"])
                    for tsem, tval in need.items():
                        if wm.get(tsem, 0) >= tval:
                            continue
                        eng.wait_ge(sems[tsem], tval)
                        wm[tsem] = tval
                    if st_["fn"] is None:
                        continue
                    inst = st_["fn"](eng)
                    if st_["inc"]:
                        inst.then_inc(sems[st_["sem"]], st_["inc"])
            return section

        block.sync(runner("sync"))
        block.vector(runner("vector"))
        block.scalar(runner("scalar"))
        block.tensor(runner("tensor"))
        block.gpsimd(runner("gpsimd"))


def build_program():
    nc = bass.Bass("TRN2", target_bir_lowering=False)

    def din(name, shape):
        return nc.dram_tensor(name, list(shape), F32, kind="ExternalInput").ap()

    X = din("X", (S, E))
    XOWN = din("XOWN", (SQ, E))
    WQT = din("WQT", (8, 128, 8, 128))
    WKT = din("WKT", (8, 128, 8, 128))
    WVT = din("WVT", (8, 128, 8, 128))
    WOT = din("WOT", (8, 128, 8, 128))
    W1T = din("W1T", (8, 2, 128, 4096))
    W2T = din("W2T", (8, 2, 128, 4096))
    BQ, BK, BV = din("BQ", (128, 8)), din("BK", (128, 8)), din("BV", (128, 8))
    BOUT = din("BOUT", (1, E))
    WG, BG = din("WG", (128, 64)), din("BG", (8, 1))
    B1, B2 = din("B1", (128, 64)), din("B2", (128, 64))
    OUT = nc.dram_tensor("OUT", [SQ, E], F32, kind="ExternalOutput").ap()
    XTH = nc.dram_tensor("XTH", [8, 128, 8, 512], BF16, kind="Internal").ap()
    XTL = nc.dram_tensor("XTL", [8, 128, 8, 512], BF16, kind="Internal").ap()
    XOH = nc.dram_tensor("XOH", [4, 128, 8, 512], BF16, kind="Internal").ap()
    XOL = nc.dram_tensor("XOL", [4, 128, 8, 512], BF16, kind="Internal").ap()
    XTO = nc.dram_tensor("XTO", [4, 128, 8, 512], F32, kind="Internal").ap()
    ATH = nc.dram_tensor("ATH", [4, 128, 8, 512], BF16, kind="Internal").ap()
    ATL = nc.dram_tensor("ATL", [4, 128, 8, 512], BF16, kind="Internal").ap()
    X1D = nc.dram_tensor("X1D", [4, 128, 8, 512], F32, kind="Internal").ap()

    ctx = ExitStack()
    pool = ctx.enter_context(nc.sbuf_tensor("pool", [128, 52960], F32))
    ident = ctx.enter_context(nc.sbuf_tensor("ident", [128, 128], F32))
    PD = [ctx.enter_context(nc.psum_tensor(f"pd{i}", [128, 1024], F32))
          for i in range(4)]
    block = ctx.enter_context(nc.Block())
    sch = Sched(nc)
    st, dma = sch.step, sch.dma

    P_ONES, P_HM = 50176, 50688
    P_BQ, P_BK, P_BV = 50692, 50700, 50708
    P_B1, P_B2, P_WG = 50716, 50780, 50844
    P_BG, P_BOUT = 50908, 50912
    P_ESEL = 51936
    SM = 48128

    def sl(off, n, p0=0, p1=128):
        return pool[p0:p1, off:off + n]

    def ones(p0, p1, n):
        return pool[p0:p1, P_ONES:P_ONES + n]

    def bsl(off, nbf, p0=0, p1=128):
        return pool[p0:p1, off:off + nbf // 2].bitcast(BF16)

    P_NEGC = P_HM + 2

    t_id0 = st("gpsimd", lambda e: e.memset(ident[:], 0.0))
    t_id = st("gpsimd", lambda e: e.affine_select(
        out=ident[:], in_=ident[:], compare_op=ALU.not_equal, fill=1.0,
        base=0, pattern=[[-1, 128]], channel_multiplier=1), waits=[t_id0])
    t_on = st("gpsimd", lambda e: e.memset(sl(P_ONES, 512), 1.0))
    t_h0 = st("gpsimd", lambda e: e.memset(sl(P_HM, 2), 0.0))
    t_h1 = st("gpsimd", lambda e: e.memset(sl(P_HM, 1, 0, 64), 1.0), waits=[t_h0])
    t_h2 = st("gpsimd", lambda e: e.memset(sl(P_HM + 1, 1, 64, 128), 1.0), waits=[t_h1])
    t_es0 = st("gpsimd", lambda e: e.memset(pool[0:8, P_ESEL:P_ESEL + 1024], 0.0))
    t_es = [t_es0, st("gpsimd", lambda e: e.affine_select(
        out=pool[0:8, P_ESEL:P_ESEL + 1024].rearrange("p (a m) -> p a m", a=8),
        in_=pool[0:8, P_ESEL:P_ESEL + 1024].rearrange("p (a m) -> p a m", a=8),
        compare_op=ALU.not_equal, fill=1.0, base=0,
        pattern=[[-1, 8], [0, 128]], channel_multiplier=1), waits=[t_es0])]
    tb = [dma("sync", dst, src, "d_c") for dst, src in (
        (sl(P_BQ, 8), BQ), (sl(P_BK, 8), BK), (sl(P_BV, 8), BV),
        (sl(P_B1, 64), B1), (sl(P_B2, 64), B2), (sl(P_WG, 64), WG),
        (sl(P_BG, 1, 0, 8), BG), (sl(P_BOUT, 1024, 0, 1), BOUT))]
    t_consts = sch.barrier("gpsimd", [t_id, t_on, t_h2] + t_es + tb)

    XIN, XSTH, XSTL, XSTF = 0, 8192, 10240, 12288
    xin_busy = [None, None]
    prev_ev = []
    prev_wr = []
    all_wr = []
    for tb_i in range(12):
        own = tb_i >= 8
        src, tb0 = (X, tb_i) if not own else (XOWN, tb_i - 8)
        bi = tb_i % 2
        xin = sl(XIN + bi * 4096, 4096)
        tl = dma("sync", xin.rearrange("p (tt d) -> p tt d", tt=4),
                 src[tb0 * 512:(tb0 + 1) * 512, :].rearrange("(tt p) d -> p tt d", p=128),
                 "d_x%d" % bi, waits=[xin_busy[bi], t_consts])
        ev_half = [None, None]
        blk_ev = []
        for tt in range(4):
            tps = []
            for dc in range(8):
                half = dc // 4
                tp = st("tensor", lambda e, tt=tt, dc=dc, half=half, xin=xin: e.transpose(
                    PD[0][:, half * 512 + (dc % 4) * 128: half * 512 + (dc % 4) * 128 + 128],
                    xin[:, tt * 1024 + dc * 128: tt * 1024 + dc * 128 + 128], ident[:]),
                    waits=[tl, t_id, ev_half[half]] + (prev_ev[-2:] if (tt == 0 and dc < 2) else []))
                tps.append(tp)
            for half in range(2):
                hi = st("scalar", lambda e, tt=tt, half=half: e.activation(
                    bsl(XSTH + half * 1024, 2048).rearrange("p (c t) -> p c t", c=4)
                      [:, :, tt * 128:tt * 128 + 128],
                    PD[0][:, half * 512:half * 512 + 512].rearrange("p (c t) -> p c t", c=4),
                    AF.Copy), waits=[tps[half * 4 + 3]] + (prev_wr if tt == 0 else []))
                if own:
                    f3 = st("vector", lambda e, tt=tt, half=half: e.tensor_copy(
                        sl(XSTF + half * 2048, 2048).rearrange("p (c t) -> p c t", c=4)
                          [:, :, tt * 128:tt * 128 + 128],
                        PD[0][:, half * 512:half * 512 + 512].rearrange("p (c t) -> p c t", c=4)),
                        waits=[hi])
                else:
                    f3 = hi
                lo = st("vector", lambda e, tt=tt, half=half: e.tensor_sub(
                    bsl(XSTL + half * 1024, 2048).rearrange("p (c t) -> p c t", c=4)
                      [:, :, tt * 128:tt * 128 + 128],
                    PD[0][:, half * 512:half * 512 + 512].rearrange("p (c t) -> p c t", c=4),
                    bsl(XSTH + half * 1024, 2048).rearrange("p (c t) -> p c t", c=4)
                      [:, :, tt * 128:tt * 128 + 128]),
                    waits=[hi, f3])
                ev_half[half] = lo
                blk_ev.append(lo)
        xin_busy[bi] = tps[-1]
        wrs = []
        dsth = XOH if own else XTH
        dstl = XOL if own else XTL
        wrs.append(dma("scalar", dsth[tb0],
                       bsl(XSTH, 4096).rearrange("p (c t) -> p c t", c=8),
                       "d_sc", waits=[blk_ev[-2], blk_ev[-1]]))
        wrs.append(dma("scalar", dstl[tb0],
                       bsl(XSTL, 4096).rearrange("p (c t) -> p c t", c=8),
                       "d_sc", waits=[blk_ev[-2], blk_ev[-1]]))
        if own:
            wrs.append(dma("scalar", XTO[tb0],
                           sl(XSTF, 4096).rearrange("p (c t) -> p c t", c=8),
                           "d_sc", waits=[blk_ev[-2], blk_ev[-1]]))
        prev_ev, prev_wr = blk_ev, wrs
        all_wr += wrs
    t_p0 = sch.barrier("gpsimd", all_wr)

    XTB = 0
    VSTG, ATMP, RBC = 8192, 8192, 10240
    AQH, AQL, AQF = 10752, 11008, 11264
    WSTG = 12288
    WS0, WS1 = 13312, 16384
    QG, KG = 19456, 21504
    VT = 25600
    KTMP, QSQ = 29760, 30272
    ABUF = 30784
    SUMS = SM

    vtok_ones = [st("gpsimd", lambda e, hh=hh: e.memset(
        pool[:, VT + hh * 2080 + 64: VT + hh * 2080 + 2080:65], 1.0),
        waits=[t_consts]) for hh in range(2)]

    attn_wr = []
    prev_group = [t_p0] + vtok_ones
    w_busy = [None, None]
    aq_busy = None
    prev_exps = None

    for g in range(NG):
        wslot = WS0 if g % 2 == 0 else WS1
        wsem = "d_w%d" % (g % 2)
        tw = []
        prev_wstg = None
        for wi, wsrc in enumerate((WQT, WKT, WVT)):
            wl = dma("sync", sl(WSTG, 1024).rearrange("p (c m) -> p c m", c=8),
                     wsrc[g], wsem,
                     waits=[w_busy[g % 2], prev_wstg, t_p0])
            sh = st("vector", lambda e, wi=wi, wslot=wslot: e.tensor_copy(
                bsl(wslot + wi * 512, 1024), sl(WSTG, 1024)), waits=[wl])
            so = st("vector", lambda e, wi=wi, wslot=wslot: e.tensor_sub(
                bsl(wslot + 1536 + wi * 512, 1024), sl(WSTG, 1024),
                bsl(wslot + wi * 512, 1024)), waits=[sh])
            tw.append(so)
            prev_wstg = so

        t_nrm0 = st("gpsimd", lambda e: e.memset(sl(SM, 2, 0, 2), 0.0),
                    waits=prev_group)
        qkv_ev = []
        bank_busy = [None] * 4
        bank_i = [0]
        norm_last = [t_nrm0]
        xt_busy = [None, None]
        qkv_last_mm = [None]
        ktmp_busy = [t_nrm0]

        def qkv_tile(src_h, src_l, t, parts, g=g, tw=tw):
            bi = t % 2
            xth = bsl(XTB + bi * 4096, 4096).rearrange("p (c k) -> p c k", c=8)
            xtl = bsl(XTB + bi * 4096 + 2048, 4096).rearrange("p (c k) -> p c k", c=8)
            tl = dma("sync", xth, src_h[t], "d_xt%d" % bi, waits=[xt_busy[bi], t_p0])
            tl2 = dma("sync", xtl, src_l[t], "d_xt%d" % bi, waits=[xt_busy[bi], t_p0])
            for (wcol, ooff, osz, bcol, wantsq) in parts:
                b = bank_i[0] % 4
                bank_i[0] += 1
                bank = PD[b // 2][:, (b % 2) * 512:(b % 2) * 512 + 512]
                mms = []
                for dc in range(8):
                    for lh, rh in ((0, xth), (0, xtl), (1536, xth)):
                        mm = st("tensor", lambda e, bank=bank, wcol=wcol, dc=dc, rh=rh,
                                lh=lh, wslot=wslot: e.matmul(
                            bank, bsl(wslot + lh + wcol + dc * 64, 128), rh[:, dc, :],
                            start=(dc == 0 and lh == 0 and rh is xth),
                            stop=(dc == 7 and lh == 1536)),
                            waits=[tl, tl2, tw[-1], bank_busy[b]] + prev_group)
                        mms.append(mm)
                if ooff == VSTG:
                    ev = st("vector", lambda e, bank=bank, t=t, bcol=bcol: e.tensor_scalar(
                        sl(VSTG + t * 512, 512), bank, sl(bcol, 1), None, ALU.add),
                        waits=[mms[-1]])
                    qkv_ev.append(ev)
                    bank_busy[b] = ev
                else:
                    ev = st("vector", lambda e, bank=bank, bcol=bcol: e.tensor_scalar(
                        sl(KTMP, 512), bank, sl(bcol, 1), None, ALU.add),
                        waits=[mms[-1]] + ktmp_busy)
                    hi = st("vector", lambda e, ooff=ooff, t=t: e.tensor_copy(
                        bsl(ooff + t * 256, 512), sl(KTMP, 512)), waits=[ev])
                    lo = st("vector", lambda e, ooff=ooff, osz=osz, t=t: e.tensor_sub(
                        bsl(ooff + osz + t * 256, 512), sl(KTMP, 512),
                        bsl(ooff + t * 256, 512)), waits=[hi])
                    qkv_ev.append(lo)
                    bank_busy[b] = ev
                    sq = st("scalar", lambda e: e.activation(
                        sl(QSQ, 512), sl(KTMP, 512), AF.Square),
                        waits=[ev, norm_last[0]])
                    nm = st("tensor", lambda e: e.matmul(
                        PD[3][0:2, 0:512], sl(P_HM, 2), sl(QSQ, 512),
                        start=True, stop=True), waits=[sq, norm_last[0]])
                    rm = st("vector", lambda e: e.reduce_max(
                        sl(SM + 2, 1, 0, 2), PD[3][0:2, 0:512], axis=AX.X), waits=[nm])
                    col = 0 if ooff == QG else 1
                    mx = st("vector", lambda e, col=col: e.tensor_max(
                        sl(SM + col, 1, 0, 2), sl(SM + col, 1, 0, 2),
                        sl(SM + 2, 1, 0, 2)), waits=[rm, t_nrm0])
                    norm_last[0] = mx
                    ktmp_busy[:] = [hi, lo, sq]
                qkv_last_mm[0] = mms[-1]
            xt_busy[bi] = qkv_last_mm[0]

        for t in range(8):
            qkv_tile(XTH, XTL, t, [(512, KG, 2048, P_BK + g, True),
                                   (1024, VSTG, 0, P_BV + g, False)])
        for t in range(4):
            qkv_tile(XOH, XOL, t, [(0, QG, 1024, P_BQ + g, True)])

        v_ev = []
        vt_busy = [None, None]
        for kc in range(NKC):
            sls = kc % 2
            soff = sls * 512
            tp = st("tensor", lambda e, kc=kc, soff=soff: e.transpose(
                PD[3][:, soff: soff + 128],
                sl(VSTG + kc * 128, 128), ident[:]),
                waits=[qkv_ev[-1], t_id, vt_busy[sls]] + ([norm_last[0]] if kc < 2 else []))
            evs = []
            for hh in range(2):
                ev = st("vector", lambda e, kc=kc, hh=hh, soff=soff: e.tensor_copy(
                    sl(VT + hh * 2080 + kc * 65, 64),
                    PD[3][:, soff + hh * 64: soff + hh * 64 + 64]),
                    waits=[tp] + vtok_ones + prev_group)
                evs.append(ev)
                v_ev.append(ev)
            vt_busy[sls] = evs[-1]

        t_q1 = st("tensor", lambda e: e.transpose(
            PD[3][0:1, 512 + 256:512 + 258], pool[0:2, SM:SM + 1], ident[0:2, 0:2]),
            waits=[norm_last[0], t_id, v_ev[-1], v_ev[-2]])
        t_k1 = st("tensor", lambda e: e.transpose(
            PD[3][0:1, 512 + 258:512 + 260], pool[0:2, SM + 1:SM + 2], ident[0:2, 0:2]),
            waits=[norm_last[0], t_id, t_q1])
        e_q1 = st("vector", lambda e: e.tensor_copy(pool[0:1, SM + 4:SM + 6],
                                                    PD[3][0:1, 512 + 256:512 + 258]),
                  waits=[t_q1])
        e_k1 = st("vector", lambda e: e.tensor_copy(pool[0:1, SM + 6:SM + 8],
                                                    PD[3][0:1, 512 + 258:512 + 260]),
                  waits=[t_k1])
        t_qk = st("vector", lambda e: e.tensor_add(pool[0:1, SM + 10:SM + 12],
                                                   pool[0:1, SM + 4:SM + 6],
                                                   pool[0:1, SM + 6:SM + 8]),
                  waits=[e_q1, e_k1])
        t_negc = st("vector", lambda e: e.tensor_scalar_mul(
            pool[0:1, SM + 10:SM + 12], pool[0:1, SM + 10:SM + 12], -CSLACK / 16.0),
            waits=[t_qk])

        prev_negc_ev = None
        for hh in range(2):
            h64 = hh * 64
            gh = 2 * g + hh
            bcm = st("tensor", lambda e, hh=hh: e.matmul(
                PD[3][:, 512 + 260 + hh: 512 + 261 + hh], ones(0, 1, 128),
                pool[0:1, SM + 10 + hh:SM + 11 + hh], start=True, stop=True),
                waits=[t_negc, t_on, t_k1, prev_negc_ev])
            negc_ev = st("vector", lambda e, hh=hh: e.tensor_copy(
                sl(P_NEGC + hh, 1), PD[3][:, 512 + 260 + hh:512 + 261 + hh]),
                waits=[bcm])
            prev_negc_ev = negc_ev

            qb_sums, last_av_ev = [], None
            prev_qb_av = {0: [], 1: []}
            for qb in range(4):
                exp_toks = []
                av_mms = []
                avb = PD[2][:, (qb % 2) * 512:(qb % 2) * 512 + 512]
                for kcb in range(16):
                    smms = []
                    for j in range(2):
                        kc = kcb * 2 + j
                        for lh, rq in ((0, 0), (0, 1024), (2048, 0)):
                            mm = st("tensor", lambda e, kc=kc, kcb=kcb, j=j, h64=h64,
                                    qb=qb, lh=lh, rq=rq: e.matmul(
                                PD[kcb % 2][:, j * 512:j * 512 + 512],
                                bsl(KG + lh + kc * 64, 128, h64, h64 + 64),
                                bsl(QG + rq + qb * 256, 512, h64, h64 + 64),
                                start=(lh == 0 and rq == 0), stop=(lh == 2048)),
                                waits=[exp_toks[kcb - 2] if kcb >= 2 else
                                       (prev_exps[14 + kcb] if (kcb < 2 and prev_exps) else None),
                                       qkv_ev[-1], negc_ev])
                            smms.append(mm)
                    if kcb >= 1:
                        pk = kcb - 1
                        for j in range(2):
                            kc = pk * 2 + j
                            avm = st("tensor", lambda e, kc=kc, hh=hh, avb=avb: e.matmul(
                                avb[0:65, :], sl(VT + hh * 2080 + kc * 65, 65),
                                sl(ABUF + kc * 512, 512),
                                start=(kc == 0), stop=(kc == 31)),
                                waits=[exp_toks[pk], v_ev[-1]] +
                                      (prev_qb_av[qb % 2] if kc == 0 else []))
                            av_mms.append(avm)
                    ex = st("scalar", lambda e, kcb=kcb, hh=hh: e.activation(
                        sl(ABUF + kcb * 1024, 1024), PD[kcb % 2][:, 0:1024],
                        AF.Exp, bias=sl(P_NEGC + hh, 1), scale=0.125),
                        waits=[smms[-1], negc_ev])
                    exp_toks.append(ex)
                for j in range(2):
                    kc = 30 + j
                    avm = st("tensor", lambda e, kc=kc, hh=hh, avb=avb: e.matmul(
                        avb[0:65, :], sl(VT + hh * 2080 + kc * 65, 65),
                        sl(ABUF + kc * 512, 512), start=False, stop=(kc == 31)),
                        waits=[exp_toks[15]])
                    av_mms.append(avm)
                evq = st("scalar", lambda e, qb=qb, avb=avb: e.activation(
                    pool[0:64, ATMP + qb * 512:ATMP + qb * 512 + 512],
                    avb[0:64, :], AF.Copy), waits=[av_mms[-1], aq_busy])
                smv = st("vector", lambda e, qb=qb, avb=avb: e.tensor_copy(
                    pool[64:65, SUMS + qb * 512:SUMS + qb * 512 + 512],
                    avb[64:65, :]), waits=[av_mms[-1], aq_busy, evq])
                qb_sums.append(smv)
                prev_qb_av[qb % 2] = [evq, smv]
                last_av_ev = evq
                prev_exps = exp_toks
            rc = st("vector", lambda e: e.reciprocal(
                pool[64:65, SUMS:SUMS + 2048], pool[64:65, SUMS:SUMS + 2048]),
                waits=qb_sums)
            aw_prev = [None]
            rb_busy = None
            for qb in range(4):
                bm = st("tensor", lambda e, qb=qb: e.matmul(
                    PD[3][:, 0:512], ones(64, 65, 128),
                    pool[64:65, SUMS + qb * 512:SUMS + qb * 512 + 512],
                    start=True, stop=True),
                    waits=[rc, t_on, rb_busy, last_av_ev])
                rbev = st("vector", lambda e: e.tensor_copy(sl(RBC, 512), PD[3][:, 0:512]),
                          waits=[bm, aw_prev[0]])
                m_hi = st("vector", lambda e, qb=qb: e.tensor_mul(
                    bsl(AQH, 512, 0, 64),
                    pool[0:64, ATMP + qb * 512:ATMP + qb * 512 + 512],
                    sl(RBC, 512, 0, 64)), waits=[rbev] + (aw_prev[0:1] if aw_prev[0] else [])
                    + [last_av_ev])
                m_f = st("vector", lambda e, qb=qb: e.tensor_mul(
                    sl(AQF, 512, 0, 64),
                    pool[0:64, ATMP + qb * 512:ATMP + qb * 512 + 512],
                    sl(RBC, 512, 0, 64)), waits=[m_hi])
                m_lo = st("vector", lambda e: e.tensor_sub(
                    bsl(AQL, 512, 0, 64), sl(AQF, 512, 0, 64), bsl(AQH, 512, 0, 64)),
                    waits=[m_f])
                w1 = dma("scalar", ATH[qb, (gh % 2) * 64:(gh % 2) * 64 + 64, gh // 2, :],
                         bsl(AQH, 512, 0, 64), "d_at", waits=[m_hi])
                w2 = dma("scalar", ATL[qb, (gh % 2) * 64:(gh % 2) * 64 + 64, gh // 2, :],
                         bsl(AQL, 512, 0, 64), "d_at", waits=[m_lo, w1])
                aw_prev[0] = w2
                rb_busy = rbev
            attn_wr.append(aw_prev[0])
            aq_busy = aw_prev[0]
        w_busy[g % 2] = qkv_last_mm[0]
        prev_group = [attn_wr[-1]]

    AR = 0
    WOS, WOH, WOL = 8192, 9216, 9728
    XO2, MBC, SBC, X1S = 10240, 11264, 13312, 15360
    RSD, SQT, GS = 17408, 33792, 34304
    ST_P2 = (38912, 40960, 43008)

    def stat(slots, i, cols=slice(0, 2048)):
        return pool[0:1, slots[i] + cols.start: slots[i] + cols.stop]

    res_ev = {}
    stat_ev = []
    sum_mms = {}
    wo_split = None
    wo_stg_busy = None
    xo_busy = {0: None, 1: None}
    ar_busy = {0: None, 1: None}
    for tt in range(4):
        arh = bsl(AR + (tt % 2) * 4096, 4096).rearrange("p (c t) -> p c t", c=8)
        arl = bsl(AR + (tt % 2) * 4096 + 2048, 4096).rearrange("p (c t) -> p c t", c=8)
        al = dma("sync", arh, ATH[tt], "d_ar%d" % (tt % 2),
                 waits=attn_wr + [ar_busy[tt % 2]])
        al2 = dma("sync", arl, ATL[tt], "d_ar%d" % (tt % 2),
                  waits=attn_wr + [ar_busy[tt % 2]])
        last_mm_set = None
        for oc in range(8):
            wl = dma("scalar", sl(WOS, 1024).rearrange("p (c m) -> p c m", c=8),
                     WOT[oc], "d_wo",
                     waits=[wo_stg_busy] if wo_stg_busy else [attn_wr[-1]])
            wh = st("vector", lambda e: e.tensor_copy(bsl(WOH, 1024), sl(WOS, 1024)),
                    waits=[wl, last_mm_set])
            wo = st("vector", lambda e: e.tensor_sub(bsl(WOL, 1024), sl(WOS, 1024),
                                                     bsl(WOH, 1024)), waits=[wh])
            wo_stg_busy = wo
            xl = dma("sync", sl(XO2 + (oc % 2) * 512, 512),
                     XTO[tt, :, oc, :], "d_xo%d" % (oc % 2),
                     waits=[xo_busy[oc % 2], t_p0])
            b = oc % 4
            bank = PD[b // 2][:, (b % 2) * 512:(b % 2) * 512 + 512]
            mms = []
            for dc in range(8):
                for pi, (lh, rh) in enumerate(((WOH, arh), (WOH, arl), (WOL, arh))):
                    mm = st("tensor", lambda e, bank=bank, dc=dc, rh=rh, lh=lh,
                            st0=(dc == 0 and pi == 0): e.matmul(
                        bank, bsl(lh + dc * 64, 128), rh[:, dc, :],
                        start=st0, stop=False),
                        waits=[al, al2, wo,
                               res_ev.get((tt, oc - 4), res_ev.get((tt - 1, oc + 4)))])
                    mms.append(mm)
            bmm = st("tensor", lambda e, bank=bank, oc=oc: e.matmul(
                bank, pool[0:1, P_BOUT + oc * 128:P_BOUT + oc * 128 + 128],
                ones(0, 1, 512), start=False, stop=True), waits=[mms[-1], t_on])
            rev = st("vector", lambda e, bank=bank, oc=oc, tt=tt: e.scalar_tensor_tensor(
                sl(RSD + oc * 2048 + tt * 512, 512), bank, 1.0,
                sl(XO2 + (oc % 2) * 512, 512), ALU.mult, ALU.add), waits=[bmm, xl])
            res_ev[(tt, oc)] = rev
            wo_busy = bmm
            xo_busy[oc % 2] = rev
            sq = st("scalar", lambda e, oc=oc, tt=tt: e.activation(
                sl(SQT, 512), sl(RSD + oc * 2048 + tt * 512, 512), AF.Square),
                waits=[rev, sum_mms.get((tt, oc - 1, 1))])
            m1 = st("tensor", lambda e, oc=oc, tt=tt: e.matmul(
                PD[2][0:1, 0:512], ones(0, 128, 1),
                sl(RSD + oc * 2048 + tt * 512, 512),
                start=(oc == 0), stop=(oc == 7)),
                waits=[rev] + ([stat_ev[-2]] if (oc == 0 and stat_ev) else []))
            m2 = st("tensor", lambda e, oc=oc: e.matmul(
                PD[2][0:1, 512:1024], ones(0, 128, 1), sl(SQT, 512),
                start=(oc == 0), stop=(oc == 7)),
                waits=[sq] + ([stat_ev[-1]] if (oc == 0 and stat_ev) else []))
            sum_mms[(tt, oc, 0)] = m1
            sum_mms[(tt, oc, 1)] = m2
            last_mm_set = bmm
        ar_busy[tt % 2] = last_mm_set
        e1 = st("vector", lambda e, tt=tt: e.tensor_copy(
            stat(ST_P2, 0, slice(tt * 512, (tt + 1) * 512)), PD[2][0:1, 0:512]),
            waits=[sum_mms[(tt, 7, 0)]])
        e2 = st("vector", lambda e, tt=tt: e.tensor_copy(
            stat(ST_P2, 1, slice(tt * 512, (tt + 1) * 512)), PD[2][0:1, 512:1024]),
            waits=[sum_mms[(tt, 7, 1)]])
        stat_ev += [e1, e2]

    def ln_stats(sts, mbc, sbc, waits, bcast_extra):
        t1 = st("vector", lambda e: e.tensor_scalar_mul(stat(sts, 0), stat(sts, 0),
                                                        1.0 / 1024), waits=waits)
        t2 = st("vector", lambda e: e.tensor_scalar_mul(stat(sts, 1), stat(sts, 1),
                                                        1.0 / 1024), waits=waits)
        t3 = st("vector", lambda e: e.tensor_mul(stat(sts, 2), stat(sts, 0),
                                                 stat(sts, 0)), waits=[t1])
        t4 = st("vector", lambda e: e.tensor_sub(stat(sts, 1), stat(sts, 1),
                                                 stat(sts, 2)), waits=[t2, t3])
        t5 = st("vector", lambda e: e.tensor_scalar_add(stat(sts, 1), stat(sts, 1),
                                                        1e-5), waits=[t4])
        t6 = st("scalar", lambda e: e.activation(stat(sts, 2), stat(sts, 1), AF.Sqrt),
                waits=[t5, t3])
        t7 = st("vector", lambda e: e.reciprocal(stat(sts, 2), stat(sts, 2)),
                waits=[t6])
        bcev = []
        prev_bm = None
        for r, dst in ((0, mbc), (2, sbc)):
            for q in range(4):
                bm = st("tensor", lambda e, r=r, q=q: e.matmul(
                    PD[3][:, 0:512], ones(0, 1, 128),
                    stat(sts, r, slice(q * 512, q * 512 + 512)),
                    start=True, stop=True),
                    waits=[t7 if r == 2 else t1, t_on,
                           bcev[-1] if bcev else None] + bcast_extra)
                ev = st("vector", lambda e, dst=dst, q=q: e.tensor_copy(
                    sl(dst + q * 512, 512), PD[3][:, 0:512]),
                    waits=[bm, bcev[-1] if bcev else None])
                bcev.append(ev)
                prev_bm = bm
        return [t7] + bcev

    ln1 = ln_stats(ST_P2, MBC, SBC, stat_ev, [attn_wr[-1]])
    x1_wr = []
    for dc in range(8):
        evs = []
        for tt in range(4):
            a = st("vector", lambda e, dc=dc, tt=tt: e.tensor_sub(
                sl(X1S + tt * 512, 512), sl(RSD + dc * 2048 + tt * 512, 512),
                sl(MBC + tt * 512, 512)),
                waits=ln1 + [res_ev[(tt, dc)], x1_wr[-1] if x1_wr else None])
            b = st("vector", lambda e, tt=tt: e.tensor_mul(
                sl(X1S + tt * 512, 512), sl(X1S + tt * 512, 512),
                sl(SBC + tt * 512, 512)), waits=[a])
            evs.append(b)
        x1_wr.append(dma("scalar", X1D[:, :, dc, :].rearrange("q p t -> p q t"),
                         sl(X1S, 2048).rearrange("p (q t) -> p q t", q=4),
                         "d_x1", waits=evs))

    t_io = st("gpsimd", lambda e: e.iota(
        sl(GS + 256, 64).bitcast(I32).rearrange("p (a b) -> p a b", a=8),
        pattern=[[1, 8], [-1, 8]], base=0, channel_multiplier=0),
        waits=[t_consts, attn_wr[-1]])
    t_iof = st("vector", lambda e: e.tensor_copy(
        sl(GS + 320, 64), sl(GS + 256, 64).bitcast(I32)), waits=[t_io])
    t_lt = st("vector", lambda e: e.tensor_single_scalar(
        sl(GS + 256, 64), sl(GS + 320, 64), 0.0, ALU.is_gt), waits=[t_iof])

    gate_sc = []
    g_ld = {0: None, 1: None}
    for tt in range(4):
        x1b = sl(AR + (tt % 2) * 4096, 4096).rearrange("p (c t) -> p c t", c=8)
        xl = dma("sync", x1b, X1D[tt], "d_ar%d" % (tt % 2),
                 waits=x1_wr + [g_ld[tt % 2] if g_ld[tt % 2] else ar_busy[tt % 2]])
        mms = []
        for dc in range(8):
            mm = st("tensor", lambda e, dc=dc, x1b=x1b: e.matmul(
                PD[0][0:8, 0:512], sl(P_WG + dc * 8, 8), x1b[:, dc, :],
                start=(dc == 0), stop=(dc == 7)),
                waits=[xl, gate_sc[-1] if gate_sc else None,
                       res_ev[(3, 4)] if (tt == 0 and dc == 0) else None,
                       res_ev[(3, 5)] if (tt == 0 and dc == 0) else None])
            mms.append(mm)
        ev = st("vector", lambda e, tt=tt: e.tensor_scalar(
            pool[0:8, GS + 512 + tt * 512:GS + 512 + (tt + 1) * 512],
            PD[0][0:8, 0:512], sl(P_BG, 1, 0, 8), None, ALU.add), waits=[mms[-1]])
        gate_sc.append(ev)
        g_ld[tt % 2] = mms[-1]
    g_tp = []
    for c in range(16):
        tp = st("tensor", lambda e, c=c: e.transpose(
            PD[1][:, c * 8:c * 8 + 8],
            pool[0:8, GS + 512 + c * 128:GS + 512 + c * 128 + 128],
            ident[0:8, 0:8]),
            waits=[gate_sc[c // 4], t_id,
                   res_ev[(3, 6)] if c == 0 else None,
                   res_ev[(3, 7)] if c == 0 else None])
        g_tp.append(tp)
    g_ev1 = st("vector", lambda e: e.tensor_copy(sl(GS, 128), PD[1][:, 0:128]),
               waits=[g_tp[-1]])
    g_ev2 = g_ev1
    sc3 = sl(GS, 128).rearrange("p (c e) -> p c e", c=16)
    r1 = st("vector", lambda e: e.tensor_tensor(
        sl(GS + 2560, 1024).rearrange("p (c a b) -> p c a b", c=16, a=8),
        sc3.unsqueeze(2).to_broadcast([128, 16, 8, 8]),
        sc3.unsqueeze(3).to_broadcast([128, 16, 8, 8]), ALU.is_gt),
        waits=[g_ev1, g_ev2, t_lt])
    r2 = st("vector", lambda e: e.tensor_tensor(
        sl(GS + 3584, 1024).rearrange("p (c a b) -> p c a b", c=16, a=8),
        sc3.unsqueeze(2).to_broadcast([128, 16, 8, 8]),
        sc3.unsqueeze(3).to_broadcast([128, 16, 8, 8]), ALU.is_equal),
        waits=[g_ev1, g_ev2, r1])
    r3 = st("vector", lambda e: e.tensor_mul(
        sl(GS + 3584, 1024).rearrange("p (c a b) -> p c a b", c=16, a=8),
        sl(GS + 3584, 1024).rearrange("p (c a b) -> p c a b", c=16, a=8),
        sl(GS + 256, 64).rearrange("p (a b) -> p a b", a=8)
          .unsqueeze(1).to_broadcast([128, 16, 8, 8])), waits=[r2])
    r4 = st("vector", lambda e: e.tensor_add(
        sl(GS + 2560, 1024), sl(GS + 2560, 1024), sl(GS + 3584, 1024)), waits=[r3])
    r5 = st("vector", lambda e: e.tensor_reduce(
        sl(GS + 128, 128).rearrange("p (c e) -> p c e", c=16),
        sl(GS + 2560, 1024).rearrange("p (c a b) -> p c a b", c=16, a=8),
        axis=AX.X, op=ALU.add), waits=[r4])
    r6 = st("vector", lambda e: e.tensor_single_scalar(
        sl(GS + 128, 128), sl(GS + 128, 128), 2.0, ALU.is_lt), waits=[r5])
    r7 = st("vector", lambda e: e.tensor_mul(
        sl(GS + 128, 128), sl(GS + 128, 128), sl(GS, 128)), waits=[r6])
    wt_tp = []
    wt_ev = []
    for c in range(16):
        rnd, pos = c // 4, c % 4
        tp = st("tensor", lambda e, c=c, rnd=rnd, pos=pos: e.transpose(
            PD[1][0:8, (rnd % 2) * 512 + pos * 128:(rnd % 2) * 512 + pos * 128 + 128],
            sl(GS + 128 + c * 8, 8), ident[:]),
            waits=[r7, t_id, wt_ev[rnd - 2] if rnd >= 2 and pos == 0 else None,
                   wt_tp[-1] if wt_tp else g_ev2])
        wt_tp.append(tp)
        if pos == 3:
            ev = st("vector", lambda e, rnd=rnd: e.tensor_copy(
                pool[0:8, SM + rnd * 512:SM + rnd * 512 + 512],
                PD[1][0:8, (rnd % 2) * 512:(rnd % 2) * 512 + 512]),
                waits=[tp, gate_sc[-1]])
            wt_ev.append(ev)
    t_wt = sch.barrier("vector", wt_ev)
    WTO = SM

    X1Q = 0
    XQH, XQL = 4096, 6144
    HTMP, F2T = 8192, 8704
    HSH, HSL = 9216, 11264
    WSC = 13312
    W1H, W1L = 15360, 19456
    W2H, W2L = 23552, 27648
    ACC = 31744

    def bsl(off, nbf, p0=0, p1=128):
        return pool[p0:p1, off:off + nbf // 2].bitcast(BF16)

    acc_ev = {}
    hs_ev = {}
    wsc_ev = {}
    prev_f1_mm = [t_wt]
    prev_f2_mm = [t_wt]
    prev_wld = None
    prev_ae = None
    prev_xsplit = None
    prev_hl = None
    for ei in range(NE):
        wsplits = []
        for wt_dram, whi, wlo in ((W1T, W1H, W1L), (W2T, W2H, W2L)):
            gate_mm = prev_f1_mm if wt_dram is W1T else prev_f2_mm
            for half in range(2):
                wl = dma("sync", sl(X1Q, 4096), wt_dram[ei, half],
                         "d_f%d" % (half % 2),
                         waits=[prev_wld if prev_wld else t_wt] +
                               (gate_mm if half == 0 else []) +
                               (wsplits[-1:] if wsplits else []))
                prev_wld = wl
                sh = st("vector", lambda e, whi=whi, half=half: e.tensor_copy(
                    bsl(whi + half * 2048, 4096), sl(X1Q, 4096)), waits=[wl])
                so = st("vector", lambda e, whi=whi, wlo=wlo, half=half: e.tensor_sub(
                    bsl(wlo + half * 2048, 4096), sl(X1Q, 4096),
                    bsl(whi + half * 2048, 4096)), waits=[sh])
                wsplits.append(so)
        for q in range(4):
            bm = st("tensor", lambda e, ei=ei, q=q: e.matmul(
                PD[3][:, 0:512], pool[0:8, P_ESEL + ei * 128:P_ESEL + ei * 128 + 128],
                pool[0:8, WTO + q * 512:WTO + q * 512 + 512],
                start=True, stop=True),
                waits=[t_wt, t_on, wsc_ev.get((ei, q - 1), wsc_ev.get((ei - 1, 3)))])
            ev = st("vector", lambda e, q=q: e.tensor_copy(
                sl(WSC + q * 512, 512), PD[3][:, 0:512]),
                waits=[bm, acc_ev.get((ei - 1, q, 7))])
            wsc_ev[(ei, q)] = ev
        for q in range(4):
            xq = sl(X1Q, 4096).rearrange("p (c t) -> p c t", c=8)
            xl = dma("sync", xq, X1D[q], "d_q",
                     waits=x1_wr + prev_f1_mm + wsplits[-1:])
            xh = st("vector", lambda e: e.tensor_copy(bsl(XQH, 4096), sl(X1Q, 4096)),
                    waits=[xl] + prev_f1_mm)
            xo = st("vector", lambda e: e.tensor_sub(bsl(XQL, 4096), sl(X1Q, 4096),
                                                     bsl(XQH, 4096)), waits=[xh])
            prev_xsplit = xo
            f1_mms = []
            for hc in range(8):
                bank = PD[0][:, (hc % 2) * 512:(hc % 2) * 512 + 512]
                mms = []
                for dc in range(8):
                    for lh, rh in ((W1H, XQH), (W1H, XQL), (W1L, XQH)):
                        mm = st("tensor", lambda e, bank=bank, hc=hc, dc=dc, lh=lh, rh=rh: e.matmul(
                            bank, bsl(lh + hc * 512 + dc * 64, 128),
                            bsl(rh + dc * 256, 512),
                            start=(dc == 0 and lh == W1H and rh == XQH),
                            stop=(dc == 7 and lh == W1L)),
                            waits=[xo, wsplits[hc * 2 + 1] if False else wsplits[-1],
                                   hs_ev.get((ei, q, hc - 2)),
                                   hs_ev.get((ei, q - 1, hc + 6)) if hc < 2 else None,
                                   hs_ev.get((ei - 1, 3, hc + 6)) if hc < 2 else None])
                        mms.append(mm)
                hv = st("scalar", lambda e, bank=bank, hc=hc, ei=ei: e.activation(
                    bsl(HSH + hc * 256, 512), bank, AF.Relu,
                    bias=sl(P_B1 + ei * 8 + hc, 1)),
                    waits=[mms[-1]] + prev_f2_mm[-1:])
                ht = st("vector", lambda e, bank=bank, hc=hc, ei=ei: e.tensor_scalar(
                    sl(HTMP, 512), bank, sl(P_B1 + ei * 8 + hc, 1), 0.0,
                    ALU.add, ALU.max), waits=[hv, prev_hl])
                hl = st("vector", lambda e, hc=hc: e.tensor_sub(
                    bsl(HSL + hc * 256, 512), sl(HTMP, 512), bsl(HSH + hc * 256, 512)),
                    waits=[ht])
                prev_hl = hl
                hs_ev[(ei, q, hc)] = hl
                f1_mms.append(mms[-1])
            prev_f1_mm = f1_mms[-2:]
            f2_mms = []
            for dc in range(8):
                bank = PD[1][:, (dc % 2) * 512:(dc % 2) * 512 + 512]
                mms = []
                for hc in range(8):
                    for lh, rh in ((W2H, HSH), (W2H, HSL), (W2L, HSH)):
                        mm = st("tensor", lambda e, bank=bank, dc=dc, hc=hc, lh=lh, rh=rh: e.matmul(
                            bank, bsl(lh + dc * 512 + hc * 64, 128),
                            bsl(rh + hc * 256, 512),
                            start=(hc == 0 and lh == W2H and rh == HSH),
                            stop=(hc == 7 and lh == W2L)),
                            waits=[hs_ev[(ei, q, hc)],
                                   acc_ev.get((ei, q, dc - 2),
                                              acc_ev.get((ei, q - 1, dc + 6),
                                                         acc_ev.get((ei - 1, 3, dc + 6))))
                                   if hc == 0 else None])
                        mms.append(mm)
                if ei == 0:
                    ae = st("vector", lambda e, bank=bank, dc=dc, q=q, ei=ei: e.scalar_tensor_tensor(
                        sl(ACC + dc * 2048 + q * 512, 512), bank,
                        sl(P_B2 + ei * 8 + dc, 1), sl(WSC + q * 512, 512),
                        ALU.add, ALU.mult), waits=[mms[-1], wsc_ev[(ei, q)]])
                else:
                    t1 = st("vector", lambda e, bank=bank, dc=dc, q=q, ei=ei: e.scalar_tensor_tensor(
                        sl(F2T, 512), bank, sl(P_B2 + ei * 8 + dc, 1),
                        sl(WSC + q * 512, 512), ALU.add, ALU.mult),
                        waits=[mms[-1], wsc_ev[(ei, q)], prev_ae])
                    ae = st("vector", lambda e, dc=dc, q=q: e.tensor_add(
                        sl(ACC + dc * 2048 + q * 512, 512),
                        sl(ACC + dc * 2048 + q * 512, 512), sl(F2T, 512)),
                        waits=[t1, acc_ev[(ei - 1, q, dc)]])
                    prev_ae = ae
                acc_ev[(ei, q, dc)] = ae
                f2_mms.append(mms[-1])
            prev_f2_mm = f2_mms[-2:]

    MBC2, SBC2 = 15360, 17408
    ST_P3b = (19456, 21504, 23552)
    X2S, OUTS = 25600, 29696
    r2_ev = {}
    st2_mms = {}
    stat2_ev = []
    for q in range(4):
        xq = sl(X1Q, 4096).rearrange("p (c t) -> p c t", c=8)
        xl = dma("sync", xq, X1D[q], "d_q",
                 waits=[acc_ev[(NE - 1, q, dc)] for dc in range(8)] +
                 prev_f1_mm + [prev_xsplit] + ([r2_ev[(q - 1, 7)]] if q else []))
        for dc in range(8):
            rv = st("vector", lambda e, dc=dc, q=q: e.tensor_add(
                sl(ACC + dc * 2048 + q * 512, 512),
                sl(ACC + dc * 2048 + q * 512, 512), xq[:, dc, :]),
                waits=[xl, acc_ev[(NE - 1, q, dc)]])
            r2_ev[(q, dc)] = rv
            sq = st("scalar", lambda e, dc=dc, q=q: e.activation(
                sl(F2T, 512), sl(ACC + dc * 2048 + q * 512, 512), AF.Square),
                waits=[rv, st2_mms.get((q, dc - 1, 1)),
                       st2_mms.get((q - 1, 7, 1)) if dc == 0 else None,
                       prev_ae if (q == 0 and dc == 0) else None])
            m1 = st("tensor", lambda e, dc=dc, q=q: e.matmul(
                PD[2][0:1, 0:512], ones(0, 128, 1),
                sl(ACC + dc * 2048 + q * 512, 512),
                start=(dc == 0), stop=(dc == 7)),
                waits=[rv] + ([stat2_ev[-2]] if (dc == 0 and stat2_ev) else
                              ([sum_mms[(3, 7, 0)]] if dc == 0 else [])))
            m2 = st("tensor", lambda e, dc=dc: e.matmul(
                PD[2][0:1, 512:1024], ones(0, 128, 1), sl(F2T, 512),
                start=(dc == 0), stop=(dc == 7)),
                waits=[sq] + ([stat2_ev[-1]] if (dc == 0 and stat2_ev) else []))
            st2_mms[(q, dc, 0)] = m1
            st2_mms[(q, dc, 1)] = m2
        e1 = st("vector", lambda e, q=q: e.tensor_copy(
            stat(ST_P3b, 0, slice(q * 512, (q + 1) * 512)), PD[2][0:1, 0:512]),
            waits=[st2_mms[(q, 7, 0)], ln1[0], wsc_ev[(NE - 1, 3)]])
        e2 = st("vector", lambda e, q=q: e.tensor_copy(
            stat(ST_P3b, 1, slice(q * 512, (q + 1) * 512)), PD[2][0:1, 512:1024]),
            waits=[st2_mms[(q, 7, 1)], ln1[0]])
        stat2_ev += [e1, e2]

    ln2 = ln_stats(ST_P3b, MBC2, SBC2, stat2_ev, [prev_f1_mm[-1], prev_f2_mm[-1]])

    out_wr = []
    otp_busy = [None, None]
    for q in range(4):
        ap_ev = []
        for dc in range(8):
            a = st("vector", lambda e, dc=dc, q=q: e.tensor_sub(
                sl(X2S + dc * 512, 512), sl(ACC + dc * 2048 + q * 512, 512),
                sl(MBC2 + q * 512, 512)),
                waits=ln2 + [r2_ev[(q, dc)]] + out_wr[-1:])
            b = st("vector", lambda e, dc=dc, q=q: e.tensor_mul(
                sl(X2S + dc * 512, 512), sl(X2S + dc * 512, 512),
                sl(SBC2 + q * 512, 512)), waits=[a])
            ap_ev.append(b)
        for stt in range(4):
            tps = []
            for dc in range(8):
                half = dc // 4
                tp = st("tensor", lambda e, dc=dc, stt=stt, half=half: e.transpose(
                    PD[3][:, half * 512 + (dc % 4) * 128: half * 512 + (dc % 4) * 128 + 128],
                    sl(X2S + dc * 512 + stt * 128, 128), ident[:]),
                    waits=[ap_ev[dc], t_id, otp_busy[half],
                           ln2[-1] if (stt == 0 and dc == 0) else None])
                tps.append(tp)
            oe = st("vector", lambda e: e.tensor_copy(sl(OUTS, 1024), PD[3][:, 0:1024]),
                    waits=[tps[3], tps[7], out_wr[-1] if out_wr else None])
            otp_busy = [oe, oe]
            ow = dma("sync", OUT[q * 512 + stt * 128: q * 512 + stt * 128 + 128, :],
                     sl(OUTS, 1024), "d_out", waits=[oe])
            out_wr.append(ow)

    sch.barrier("sync", out_wr)
    sch.barrier("gpsimd", [out_wr[-1]])
    sch.emit(block, ctx)
    ctx.close()
    return nc


_NC = None


def kernel(**inputs):
    global _NC
    X = np.ascontiguousarray(np.asarray(inputs["L0_attn_X"], dtype=np.float32))
    Wqkv = np.asarray(inputs["L0_attn_W_qkv"], dtype=np.float32)
    bqkv = np.asarray(inputs["L0_attn_b_qkv"], dtype=np.float32)
    Wout = np.asarray(inputs["L0_attn_W_out"], dtype=np.float32)
    bout = np.asarray(inputs["L0_attn_b_out"], dtype=np.float32)
    Wg = np.asarray(inputs["L0_ffn_W_gate"], dtype=np.float32)
    bg = np.asarray(inputs["L0_ffn_b_gate"], dtype=np.float32)
    W1 = np.asarray(inputs["L0_ffn_W1"], dtype=np.float32)
    b1 = np.asarray(inputs["L0_ffn_b1"], dtype=np.float32)
    W2 = np.asarray(inputs["L0_ffn_W2"], dtype=np.float32)
    b2 = np.asarray(inputs["L0_ffn_b2"], dtype=np.float32)

    idx = (np.arange(H)[:, None] * 192 + np.arange(DH)[None, :]).ravel()
    WQ, WK, WV = Wqkv[:, idx], Wqkv[:, idx + 64], Wqkv[:, idx + 128]

    def t2(w):
        return np.ascontiguousarray(w.reshape(8, 128, 8, 128).transpose(2, 1, 0, 3))

    def t3(w):
        return np.ascontiguousarray(
            w.reshape(8, 8, 128, 2, 4, 128).transpose(0, 3, 2, 4, 1, 5)
            .reshape(8, 2, 128, 4096))

    shared = dict(
        WQT=t2(WQ), WKT=t2(WK), WVT=t2(WV), WOT=t2(Wout),
        W1T=t3(W1), W2T=t3(W2),
        BQ=np.ascontiguousarray(bqkv[idx].reshape(8, 128).T),
        BK=np.ascontiguousarray(bqkv[idx + 64].reshape(8, 128).T),
        BV=np.ascontiguousarray(bqkv[idx + 128].reshape(8, 128).T),
        BOUT=bout.reshape(1, E),
        WG=np.ascontiguousarray(Wg.reshape(8, 128, 8).transpose(1, 0, 2).reshape(128, 64)),
        BG=bg.reshape(8, 1),
        B1=np.ascontiguousarray(b1.reshape(8, 8, 128).transpose(2, 0, 1).reshape(128, 64)),
        B2=np.ascontiguousarray(b2.reshape(8, 8, 128).transpose(2, 0, 1).reshape(128, 64)),
    )
    if _NC is None:
        _NC = build_program()
    in_maps = []
    for c in range(8):
        b, half = c // 2, c % 2
        m = dict(shared)
        m["X"] = X[b]
        m["XOWN"] = np.ascontiguousarray(X[b, half * SQ:(half + 1) * SQ, :])
        in_maps.append(m)
    res = run_bass_kernel_spmd(_NC, in_maps, core_ids=list(range(8)))
    out = np.empty((4, S, E), dtype=np.float32)
    for c in range(8):
        b, half = c // 2, c % 2
        out[b, half * SQ:(half + 1) * SQ, :] = res.results[c]["OUT"]
    return out
